# version 04
# baseline (speedup 1.0000x reference)
import math
import numpy as np
from contextlib import ExitStack

import concourse.bass as bass
import concourse.mybir as mybir
from concourse.bass_utils import run_bass_kernel_spmd

F32 = mybir.dt.float32
BF16 = mybir.dt.bfloat16
AF = mybir.ActivationFunctionType
ALU = mybir.AluOpType
AX = mybir.AxisListType

NCORES = 8
D = 1024
S = 8192
TOK = 2048
NCH = D // 128

_COMPILED = {}


class KB:
    ENGS = ("sync", "tensor", "vector", "scalar", "gpsimd")
    DMA_QUEUES = ("sync", "scalar", "gpsimd")
    NLANES = 6

    def __init__(self, name="k"):
        self.nc = bass.Bass("TRN2", target_bir_lowering=False)
        self.es = ExitStack()
        self.streams = {e: [] for e in self.ENGS}
        self._sbid = 0
        self.esem = {e: self._mk_sem(f"m_{e}") for e in self.ENGS}
        self.ecount = {e: 0 for e in self.ENGS}
        self.lanes = {q: [self._mk_sem(f"L_{q}{i}") for i in range(self.NLANES)]
                      for q in self.DMA_QUEUES}
        self.lane_uses = {q: [0] * self.NLANES for q in self.DMA_QUEUES}
        self.lane_rr = {q: 0 for q in self.DMA_QUEUES}
        self.semh = {}
        for e in self.ENGS:
            self.semh[f"m_{e}"] = self.esem[e]
        for q in self.DMA_QUEUES:
            for i in range(self.NLANES):
                self.semh[f"L_{q}{i}"] = self.lanes[q][i]
        self.last_w = {}
        self.readers = {}
        self.waited = {e: {} for e in self.ENGS}

    def _mk_sem(self, name):
        return self.es.enter_context(self.nc.semaphore(name))

    def din(self, name, shape, dtype=F32):
        return self.nc.dram_tensor(name, list(shape), dtype, kind="ExternalInput").ap()

    def dout(self, name, shape, dtype=F32):
        return self.nc.dram_tensor(name, list(shape), dtype, kind="ExternalOutput").ap()

    def dtmp(self, name, shape, dtype=F32):
        return self.nc.dram_tensor(name, list(shape), dtype, kind="Internal").ap()

    def sb(self, shape, dtype=F32, name=None):
        self._sbid += 1
        return self.es.enter_context(
            self.nc.sbuf_tensor(name or f"sb{self._sbid}", list(shape), dtype))

    def psum(self, shape, dtype=F32, name=None):
        self._sbid += 1
        return self.es.enter_context(
            self.nc.psum_tensor(name or f"ps{self._sbid}", list(shape), dtype))

    @staticmethod
    def _key(ap_or_handle):
        t = getattr(ap_or_handle, "tensor", ap_or_handle)
        return getattr(t, "name", None) or id(t)

    @staticmethod
    def _box(ap):
        t = getattr(ap, "tensor", None)
        if t is None:
            return None
        try:
            shape = t.shape
            if len(shape) < 2:
                return None
            fext = 1
            for s_ in shape[1:]:
                fext *= int(s_)
            off = ap.offset
            if not isinstance(off, int):
                return None
            apl = list(ap.ap)
            if len(apl) != 2:
                return None
            (pst, pcnt), (fst, fcnt) = apl
            if int(pst) != fext or int(fst) != 1:
                return None
            p0 = off // fext
            f0 = off % fext
            return (p0, p0 + int(pcnt), f0, f0 + int(fcnt))
        except Exception:
            return None

    @staticmethod
    def _overlap(a, b):
        if a is None or b is None:
            return True
        return not (a[1] <= b[0] or b[1] <= a[0] or a[3] <= b[2] or b[3] <= a[2])

    MAXENT = 24

    def _entries(self, d, kk):
        return d.setdefault(kk, [])

    def _compact(self, lst):
        if len(lst) <= self.MAXENT:
            return lst
        best = {}
        for box, sname, val in lst:
            if best.get(sname, -1) < val:
                best[sname] = val
        return [(None, s, v) for s, v in best.items()]

    def _need(self, eng, reads, writes, pe_acc=False):
        need = {}

        def add(sname, val):
            if pe_acc and eng == "tensor" and sname == "m_tensor":
                return
            if need.get(sname, 0) < val:
                need[sname] = val

        for ap in reads:
            kk = self._key(ap)
            bx = self._box(ap)
            for box, sname, val in self.last_w.get(kk, ()):
                if self._overlap(bx, box):
                    add(sname, val)
        for ap in writes:
            kk = self._key(ap)
            bx = self._box(ap)
            for box, sname, val in self.last_w.get(kk, ()):
                if self._overlap(bx, box):
                    add(sname, val)
            for box, sname, val in self.readers.get(kk, ()):
                if self._overlap(bx, box):
                    add(sname, val)
        out = []
        for sname, val in need.items():
            w = self.waited[eng].get(sname, -1)
            if val > w:
                out.append((sname, val))
                self.waited[eng][sname] = val
        return out

    def _record(self, reads, writes, prod):
        sname, val = prod
        for ap in reads:
            kk = self._key(ap)
            lst = self._entries(self.readers, kk)
            lst.append((self._box(ap), sname, val))
            self.readers[kk] = self._compact(lst)
        for ap in writes:
            kk = self._key(ap)
            bx = self._box(ap)
            if bx is None:
                self.last_w[kk] = [(None, sname, val)]
                self.readers[kk] = []
            else:
                kept_w = [e for e in self.last_w.get(kk, ())
                          if not self._covers(bx, e[0])]
                kept_w.append((bx, sname, val))
                self.last_w[kk] = self._compact(kept_w)
                self.readers[kk] = [e for e in self.readers.get(kk, ())
                                    if not self._covers(bx, e[0])]

    @staticmethod
    def _covers(bx, other):
        if other is None:
            return False
        return (bx[0] <= other[0] and bx[1] >= other[1] and
                bx[2] <= other[2] and bx[3] >= other[3])

    def op(self, eng, method, reads=(), writes=(), pe_acc=False, **kwargs):
        waits = self._need(eng, reads, writes, pe_acc=pe_acc)
        self.ecount[eng] += 1
        tgt = self.ecount[eng]
        wait_hs = tuple((self.semh[s], v) for s, v in waits)
        sem_h = self.esem[eng]

        def fn(e, method=method, kwargs=kwargs, wait_hs=wait_hs, sem_h=sem_h):
            for h, v in wait_hs:
                e.wait_ge(h, v)
            getattr(e, method)(**kwargs).then_inc(sem_h, 1)

        self.streams[eng].append(fn)
        self._record(reads, writes, (f"m_{eng}", tgt))
        return tgt

    def dma(self, q, out, in_):
        reads, writes = (in_,), (out,)
        waits = self._need(q, reads, writes)
        li = self.lane_rr[q]
        self.lane_rr[q] = (li + 1) % self.NLANES
        lane_sem = self.lanes[q][li]
        lname = f"L_{q}{li}"
        prev = self.lane_uses[q][li] * 16
        if prev > self.waited[q].get(lname, -1):
            waits = list(waits) + [(lname, prev)]
            self.waited[q][lname] = prev
        self.lane_uses[q][li] += 1
        tgt = self.lane_uses[q][li] * 16
        wait_hs = tuple((self.semh[s], v) for s, v in waits)

        def fn(e, out=out, in_=in_, wait_hs=wait_hs, lane_sem=lane_sem):
            for h, v in wait_hs:
                e.wait_ge(h, v)
            e.dma_start(out=out, in_=in_).then_inc(lane_sem, 16)

        self.streams[q].append(fn)
        self._record(reads, writes, (lname, tgt))
        return tgt

    def full_barrier(self):
        vals = ([(f"m_{e}", self.ecount[e]) for e in self.ENGS if self.ecount[e] > 0] +
                [(f"L_{q}{i}", self.lane_uses[q][i] * 16)
                 for q in self.DMA_QUEUES for i in range(self.NLANES)
                 if self.lane_uses[q][i] > 0])
        for eng in self.ENGS:
            pairs = []
            for sname, v in vals:
                if v > self.waited[eng].get(sname, -1):
                    pairs.append((self.semh[sname], v))
                    self.waited[eng][sname] = v
            if pairs:
                def fn(e, pairs=tuple(pairs)):
                    for h, v in pairs:
                        e.wait_ge(h, v)
                self.streams[eng].append(fn)

    class _Scope:
        def __init__(self, kb):
            self.kb = kb
            self.es = ExitStack()

        def sb(self, shape, dtype=F32, name=None):
            self.kb._sbid += 1
            return self.es.enter_context(self.kb.nc.sbuf_tensor(
                name or f"sb{self.kb._sbid}", list(shape), dtype))

        def __enter__(self):
            return self

        def __exit__(self, *a):
            self.es.close()
            self.kb.full_barrier()
            return False

    def scope(self):
        return KB._Scope(self)

    def raw(self, eng, fn):
        self.streams[eng].append(fn)

    def barrier_final(self, eng="gpsimd"):
        vals = ([(self.esem[e], self.ecount[e]) for e in self.ENGS if self.ecount[e] > 0] +
                [(self.lanes[q][i], self.lane_uses[q][i] * 16)
                 for q in self.DMA_QUEUES for i in range(self.NLANES)
                 if self.lane_uses[q][i] > 0])

        def fn(e, vals=vals):
            for h, v in vals:
                e.wait_ge(h, v)

        self.streams[eng].append(fn)

    def mm(self, out, lhsT, rhs, start, stop):
        self.op("tensor", "matmul", reads=(lhsT, rhs), writes=(out,),
                pe_acc=not start, out=out, lhsT=lhsT, rhs=rhs, start=start, stop=stop)

    def act(self, out, in_, func, bias=None, scale=None, accum_out=None, eng="scalar"):
        kw = dict(out=out, in_=in_, func=func)
        reads = [in_]
        if isinstance(bias, float) and bias != 0.0:
            bias = self.constcol(bias, out.shape[0])
        if bias is not None:
            kw["bias"] = bias
            if not isinstance(bias, float):
                reads.append(bias)
        if scale is not None:
            kw["scale"] = scale
            if not isinstance(scale, float):
                reads.append(scale)
        writes = [out]
        if accum_out is not None:
            kw["accum_out"] = accum_out
            writes.append(accum_out)
        self.op(eng, "activation", reads=tuple(reads), writes=tuple(writes), **kw)

    def constcol(self, value, parts=128):
        key = (float(value), parts)
        if not hasattr(self, "_consts"):
            self._consts = {}
        if key not in self._consts:
            buf = self.sb((parts, 1), F32, f"const_{len(self._consts)}")
            self.memset(buf[:, :], float(value))
            self._consts[key] = buf[:, :]
        return self._consts[key]

    def copy(self, out, in_, eng="scalar"):
        if eng == "scalar":
            self.op(eng, "copy", reads=(in_,), writes=(out,), out=out, in_=in_)
        else:
            self.op(eng, "tensor_copy", reads=(in_,), writes=(out,), out=out, in_=in_)

    def tt(self, out, a, b, alu, eng="vector"):
        self.op(eng, "tensor_tensor", reads=(a, b), writes=(out,),
                out=out, in0=a, in1=b, op=alu)

    def stt(self, out, in0, scalar, in1, op0, op1, accum_out=None, eng="vector"):
        reads = [in0, in1]
        if not isinstance(scalar, float):
            reads.append(scalar)
        writes = [out]
        kw = dict(out=out, in0=in0, scalar=scalar, in1=in1, op0=op0, op1=op1)
        if accum_out is not None:
            kw["accum_out"] = accum_out
            writes.append(accum_out)
        self.op(eng, "scalar_tensor_tensor", reads=tuple(reads), writes=tuple(writes), **kw)

    def tsc(self, out, in0, s1, op0, s2=None, op1=None, accum_out=None, eng="vector"):
        reads = [in0]
        for s in (s1, s2):
            if s is not None and not isinstance(s, float):
                reads.append(s)
        writes = [out]
        kw = dict(out=out, in0=in0, scalar1=s1, scalar2=s2, op0=op0)
        if op1 is not None:
            kw["op1"] = op1
        if accum_out is not None:
            kw["accum_out"] = accum_out
            writes.append(accum_out)
        self.op(eng, "tensor_scalar", reads=tuple(reads), writes=tuple(writes), **kw)

    def memset(self, ap, val, eng="vector"):
        self.op(eng, "memset", reads=(), writes=(ap,), ap=ap, constant=val)

    def reduce(self, out, in_, alu, eng="vector", negate=False):
        self.op(eng, "tensor_reduce", reads=(in_,), writes=(out,),
                out=out, in_=in_, axis=AX.X, op=alu, negate=negate)

    def finalize(self):
        self.barrier_final()
        block = self.es.enter_context(self.nc.Block())
        for e in self.ENGS:
            fns = self.streams[e]
            if fns:
                getattr(block, e)(lambda eng, fns=fns: [f(eng) for f in fns])
        self.es.close()
        return self.nc


def cdiv(a, b):
    return (a + b - 1) // b


def build_F():
    k = KB("F")
    xf = k.din("xf", (D, S), F32)
    gw1T = k.din("gw1T", (D, D), F32)
    gb1 = k.din("gb1", (128, 8), F32)
    gw2T = k.din("gw2T", (D, 8), F32)
    gb2 = k.din("gb2", (8, 1), F32)
    w1Tq = k.din("w1Tq", (D, D), F32)
    tw1Tq = k.din("tw1Tq", (8, D, D), F32)
    b1q = k.din("b1q", (128, 8), F32)
    tb1q = k.din("tb1q", (128, 64), F32)
    w2Tq = k.din("w2Tq", (D, D), F32)
    tw2Tq = k.din("tw2Tq", (8, D, D), F32)
    b2h = k.din("b2h", (128, 8), F32)
    tb2 = k.din("tb2", (128, 64), F32)
    addb2 = k.din("addb2", (1, 1), F32)
    part = k.dout("part", (D, S), F32)

    xln_dram = k.dtmp("xln_dram", (D, S), BF16)
    g_dram = k.dtmp("g_dram", (8, 1), F32)

    NT = 512
    ntiles = S // NT

    xstage = k.sb((128, 8 * NT), F32, "xstage")
    xbf = k.sb((128, 8 * NT), BF16, "xbf")
    ln = LNHelper(k, maxN=NT)
    gw1_bf = k.sb((128, 8 * D), BF16, "gw1_bf")
    gw2_bf = k.sb((128, 8 * 8), BF16, "gw2_bf")
    wstage = k.sb((128, D), F32, "wstage")
    hbuf = k.sb((128, 8 * NT), BF16, "hbuf")
    ps_a = [k.psum((128, 512), F32, f"psa{i}") for i in range(4)]
    ps_g = k.psum((8, NT), F32, "ps_g")
    gacc = k.sb((8, 1), F32, "gacc")
    gred = k.sb((8, 1), F32, "gred")
    gcol_all = k.sb((128, 8), F32, "gcol_all")
    grow = k.sb((1, 8), F32, "grow")
    ones_row = k.sb((1, 128), F32, "ones_row")

    k.memset(gacc[:, :], 0.0)
    for kc in range(8):
        k.dma("sync", wstage[:, :], gw1T[kc * 128:(kc + 1) * 128, :])
        k.copy(gw1_bf[:, kc * D:(kc + 1) * D], wstage[:, :], eng="vector")
        k.dma("sync", wstage[:, 0:8], gw2T[kc * 128:(kc + 1) * 128, :])
        k.copy(gw2_bf[:, kc * 8:(kc + 1) * 8], wstage[:, 0:8], eng="vector")
    gb1c = k.sb((128, 8), F32, "gb1c")
    k.dma("sync", gb1c[:, :], gb1[:, :])

    pe = None
    for t in range(ntiles):
        xs = [xstage[:, c * NT:(c + 1) * NT] for c in range(8)]
        xbs = [xbf[:, c * NT:(c + 1) * NT] for c in range(8)]
        for c in range(8):
            k.dma("sync", xs[c], xf[c * 128:(c + 1) * 128, t * NT:(t + 1) * NT])
            k.copy(xbs[c], xs[c], eng="vector")
        ln.stats(k, xs, xbs, NT)
        for c in range(8):
            ln.apply(k, xbs[c], xs[c], NT)
            k.dma("gpsimd", xln_dram[c * 128:(c + 1) * 128, t * NT:(t + 1) * NT], xbs[c])
        for oc in range(8):
            p = ps_a[oc % 4]
            for kc in range(8):
                k.mm(p[:, :NT], gw1_bf[:, kc * D + oc * 128: kc * D + (oc + 1) * 128],
                     xbs[kc], start=(kc == 0), stop=(kc == 7))
            k.act(hbuf[:, oc * NT:(oc + 1) * NT], p[:, :NT], AF.Relu,
                  bias=gb1c[:, oc:oc + 1])
        for kc in range(8):
            k.mm(ps_g[:, :], gw2_bf[:, kc * 8:(kc + 1) * 8],
                 hbuf[:, kc * NT:(kc + 1) * NT], start=(kc == 0), stop=(kc == 7))
        k.reduce(gred[:, :], ps_g[:, :], ALU.add)
        k.tt(gacc[:, :], gacc[:, :], gred[:, :], ALU.add)

    gb2c = k.sb((8, 1), F32, "gb2c")
    k.dma("sync", gb2c[:, :], gb2[:, :])
    k.tsc(gacc[:, :], gacc[:, :], 1.0 / S, ALU.mult)
    k.tt(gacc[:, :], gacc[:, :], gb2c[:, :], ALU.add)
    k.dma("gpsimd", g_dram[:, :], gacc[:, :])
    k.dma("gpsimd", grow[:, :], g_dram.rearrange("e one -> one e"))
    k.memset(ones_row[:, :], 1.0)
    ps_b128 = ln.ps_b
    k.mm(ps_b128[:, 0:8], ones_row[:, :], grow[:, :], start=True, stop=True)
    k.copy(gcol_all[:, :], ps_b128[:, 0:8], eng="scalar")

    mw1 = k.sb((128, 8 * D), BF16, "mw1")
    mw2 = k.sb((128, 8 * D), BF16, "mw2")
    wacc = k.sb((128, D), F32, "wacc")
    for (dst, wsrc, twsrc) in ((mw1, w1Tq, tw1Tq), (mw2, w2Tq, tw2Tq)):
        for kc in range(8):
            k.dma("sync", wacc[:, :], wsrc[kc * 128:(kc + 1) * 128, :])
            for e in range(8):
                k.dma("sync", wstage[:, :], twsrc[e, kc * 128:(kc + 1) * 128, :])
                k.stt(wacc[:, :], wstage[:, :], gcol_all[:, e:e + 1], wacc[:, :],
                      ALU.mult, ALU.add)
            k.copy(dst[:, kc * D:(kc + 1) * D], wacc[:, :], eng="vector")

    mb1c = k.sb((128, 8), F32, "mb1c")
    tb1stage = k.sb((128, 8 * 8), F32, "tb1stage")
    k.dma("sync", mb1c[:, :], b1q[:, :])
    k.dma("sync", tb1stage[:, :], tb1q[:, :])
    for e in range(8):
        k.stt(mb1c[:, :], tb1stage[:, e * 8:(e + 1) * 8], gcol_all[:, e:e + 1],
              mb1c[:, :], ALU.mult, ALU.add)
    mb2c = k.sb((128, 8), F32, "mb2c")
    tb2stage = k.sb((128, 8 * 8), F32, "tb2stage")
    k.dma("sync", mb2c[:, :], b2h[:, :])
    k.dma("sync", tb2stage[:, :], tb2[:, :])
    for e in range(8):
        k.stt(mb2c[:, :], tb2stage[:, e * 8:(e + 1) * 8], gcol_all[:, e:e + 1],
              mb2c[:, :], ALU.mult, ALU.add)
    addb2c = k.sb((1, 1), F32, "addb2c")
    k.dma("sync", addb2c[:, :], addb2[:, :])
    addb2_all = k.sb((128, 1), F32, "addb2_all")
    k.mm(ps_b128[:, 0:1], ones_row[:, :], addb2c[:, :], start=True, stop=True)
    k.copy(addb2_all[:, :], ps_b128[:, 0:1], eng="scalar")
    halfmb2 = k.sb((128, 8), F32, "halfmb2")
    k.tsc(mb2c[:, :], mb2c[:, :], addb2_all[:, 0:1], ALU.mult)
    k.tsc(halfmb2[:, :], mb2c[:, :], 0.5, ALU.mult)

    ubuf = k.sb((128, 8 * NT), BF16, "ubuf")
    outstage = k.sb((128, 8 * NT), F32, "outstage")
    for t in range(ntiles):
        xbs = [xbf[:, c * NT:(c + 1) * NT] for c in range(8)]
        for c in range(8):
            k.dma("sync", xbs[c], xln_dram[c * 128:(c + 1) * 128, t * NT:(t + 1) * NT])
        for fc in range(8):
            p = ps_a[fc % 4]
            for kc in range(8):
                k.mm(p[:, :NT], mw1[:, kc * D + fc * 128: kc * D + (fc + 1) * 128],
                     xbs[kc], start=(kc == 0), stop=(kc == 7))
            k.act(ubuf[:, fc * NT:(fc + 1) * NT], p[:, :NT], AF.Gelu_apprx_tanh,
                  bias=mb1c[:, fc:fc + 1])
        for hc in range(8):
            p = ps_a[hc % 4]
            for fc in range(8):
                k.mm(p[:, :NT], mw2[:, fc * D + hc * 128: fc * D + (hc + 1) * 128],
                     ubuf[:, fc * NT:(fc + 1) * NT], start=(fc == 0), stop=(fc == 7))
            k.stt(outstage[:, hc * NT:(hc + 1) * NT], p[:, :NT], 0.5,
                  halfmb2[:, hc:hc + 1].to_broadcast([128, NT]), ALU.mult, ALU.add)
        for hc in range(8):
            k.dma("gpsimd", part[hc * 128:(hc + 1) * 128, t * NT:(t + 1) * NT],
                  outstage[:, hc * NT:(hc + 1) * NT])
    return k.finalize()


class LNHelper:
    def __init__(self, k, maxN=512, name="ln", ps_b=None, tmp=None):
        self.maxN = maxN
        self._ext_tmp = tmp
        self.ones_col_bf = k.sb((128, 1), BF16, f"{name}_oc")
        self.ones_row_f = k.sb((1, 128), F32, f"{name}_or")
        self.sq_bf = k.sb((128, maxN), BF16, f"{name}_sq")
        self.ps_s = k.psum((1, maxN), F32, f"{name}_ps_s")
        self.ps_q = k.psum((1, maxN), F32, f"{name}_ps_q")
        self.ps_b = ps_b if ps_b is not None else k.psum((128, maxN), F32, f"{name}_ps_b")
        self.row = k.sb((1, 3 * maxN), F32, f"{name}_row")
        self.mu_b = k.sb((128, maxN), F32, f"{name}_mu")
        self.rstd_b = k.sb((128, maxN), F32, f"{name}_rs")
        self.tmp = tmp if tmp is not None else k.sb((128, maxN), F32, f"{name}_tmp")
        k.memset(self.ones_col_bf[:, :], 1.0)
        k.memset(self.ones_row_f[:, :], 1.0)

    def stats_chunk(self, k, c, nc_total, x_f32, x_bf16, N):
        k.mm(self.ps_s[:, :N], self.ones_col_bf[:, :], x_bf16,
             start=(c == 0), stop=(c == nc_total - 1))
        k.act(self.sq_bf[:, :N], x_f32, AF.Square)
        k.mm(self.ps_q[:, :N], self.ones_col_bf[:, :], self.sq_bf[:, :N],
             start=(c == 0), stop=(c == nc_total - 1))

    def finish_stats(self, k, N, eps=1e-5, nfeat=1024):
        mu = self.row[:, 0:N]
        var = self.row[:, self.maxN:self.maxN + N]
        rstd = var
        t2 = self.row[:, 2 * self.maxN:2 * self.maxN + N]
        inv_n = 1.0 / nfeat
        k.act(mu, self.ps_s[:, :N], AF.Copy, scale=inv_n)
        k.act(t2, mu, AF.Square)
        k.act(var, self.ps_q[:, :N], AF.Copy, scale=inv_n)
        k.tt(var, var, t2, ALU.subtract)
        k.act(rstd, var, AF.Ln, bias=eps)
        k.act(rstd, rstd, AF.Exp, scale=-0.5)
        k.mm(self.ps_b[:, :N], self.ones_row_f[:, :], mu, start=True, stop=True)
        k.copy(self.mu_b[:, :N], self.ps_b[:, :N], eng="scalar")
        k.mm(self.ps_b[:, :N], self.ones_row_f[:, :], rstd, start=True, stop=True)
        k.copy(self.rstd_b[:, :N], self.ps_b[:, :N], eng="scalar")

    def stats(self, k, x_chunks_f32, xbf_chunks, N, eps=1e-5, nfeat=1024):
        nc = len(x_chunks_f32)
        for c in range(nc):
            k.mm(self.ps_s[:, :N], self.ones_col_bf[:, :], xbf_chunks[c],
                 start=(c == 0), stop=(c == nc - 1))
        for c in range(nc):
            k.act(self.sq_bf[:, :N], x_chunks_f32[c], AF.Square)
            k.mm(self.ps_q[:, :N], self.ones_col_bf[:, :], self.sq_bf[:, :N],
                 start=(c == 0), stop=(c == nc - 1))
        mu = self.row[:, 0:N]
        var = self.row[:, self.maxN:self.maxN + N]
        rstd = var
        t2 = self.row[:, 2 * self.maxN:2 * self.maxN + N]
        inv_n = 1.0 / nfeat
        k.act(mu, self.ps_s[:, :N], AF.Copy, scale=inv_n)
        k.act(t2, mu, AF.Square)
        k.act(var, self.ps_q[:, :N], AF.Copy, scale=inv_n)
        k.tt(var, var, t2, ALU.subtract)
        k.act(rstd, var, AF.Ln, bias=eps)
        k.act(rstd, rstd, AF.Exp, scale=-0.5)
        k.mm(self.ps_b[:, :N], self.ones_row_f[:, :], mu, start=True, stop=True)
        k.copy(self.mu_b[:, :N], self.ps_b[:, :N], eng="scalar")
        k.mm(self.ps_b[:, :N], self.ones_row_f[:, :], rstd, start=True, stop=True)
        k.copy(self.rstd_b[:, :N], self.ps_b[:, :N], eng="scalar")

    def apply(self, k, out_bf, x_f32, N, gcol=None, bcol=None):
        k.tt(self.tmp[:, :N], x_f32, self.mu_b[:, :N], ALU.subtract)
        if gcol is None:
            k.tt(out_bf, self.tmp[:, :N], self.rstd_b[:, :N], ALU.mult)
        else:
            k.tt(self.tmp[:, :N], self.tmp[:, :N], self.rstd_b[:, :N], ALU.mult)
            k.stt(out_bf, self.tmp[:, :N], gcol, bcol, ALU.mult, ALU.add)


def _run(name, builder, in_maps):
    if name not in _COMPILED:
        _COMPILED[name] = builder()
    nc = _COMPILED[name]
    res = run_bass_kernel_spmd(nc, in_maps, core_ids=list(range(NCORES)))
    return res.results


def _fm(a):
    a = np.asarray(a, dtype=np.float32)
    if a.ndim == 3:
        a = a.reshape(-1, a.shape[-1])
    return np.ascontiguousarray(a.T)


def stage_F(x_fm, W):
    maps = []
    for c in range(NCORES):
        b, fq = c // 4, c % 4
        sl = slice(fq * 1024, (fq + 1) * 1024)
        m = {
            "xf": x_fm[b],
            "gw1T": np.ascontiguousarray(W["L0_ffn_gate_w1"].T),
            "gb1": np.asarray(W["L0_ffn_gate_b1"]).reshape(8, 128).T,
            "gw2T": np.ascontiguousarray(W["L0_ffn_gate_w2"].T),
            "gb2": W["L0_ffn_gate_b2"].reshape(-1, 1),
            "w1Tq": np.ascontiguousarray(W["L0_ffn_W1"][sl, :].T),
            "tw1Tq": np.ascontiguousarray(np.transpose(W["L0_ffn_TW1"][:, sl, :], (0, 2, 1))),
            "b1q": np.asarray(W["L0_ffn_b1"])[sl].reshape(8, 128).T,
            "tb1q": np.transpose(np.asarray(W["L0_ffn_Tb1"])[:, sl].reshape(8, 8, 128), (2, 0, 1)).reshape(128, 64),
            "w2Tq": np.ascontiguousarray(W["L0_ffn_W2"][:, sl].T),
            "tw2Tq": np.ascontiguousarray(np.transpose(W["L0_ffn_TW2"][:, :, sl], (0, 2, 1))),
            "b2h": np.asarray(W["L0_ffn_b2"]).reshape(8, 128).T,
            "tb2": np.transpose(np.asarray(W["L0_ffn_Tb2"]).reshape(8, 8, 128), (2, 0, 1)).reshape(128, 64),
            "addb2": np.array([[1.0 if fq == 0 else 0.0]], dtype=np.float32),
        }
        maps.append({kk: np.ascontiguousarray(vv, dtype=np.float32) for kk, vv in m.items()})
    return maps


def run_F(x_fm, W):
    maps = stage_F(x_fm, W)
    res = _run("F", build_F, maps)
    return [[res[b * 4 + q]["part"] for q in range(4)] for b in range(2)]





def build_AK():
    k = KB("AK")
    xbase = k.din("xbase", (D, TOK), F32)
    parts = [k.din(f"p{i}", (D, TOK), F32) for i in range(4)]
    wkT = k.din("wkT", (D, D), F32)
    wvT = k.din("wvT", (D, D), F32)
    bk = k.din("bk", (128, 8), F32)
    bv = k.din("bv", (1, D), F32)
    x1own = k.dout("x1own", (D, TOK), F32)
    xlnown = k.dout("xlnown", (D, TOK), BF16)
    kown = k.dout("kown", (D, TOK), BF16)
    vown = k.dout("vown", (TOK, D), BF16)

    NT = 512
    ln = LNHelper(k, maxN=NT, name="AKln")
    xstage = k.sb((128, 8 * NT), F32, "AK_xst")
    xtile2 = k.sb((128, 8 * NT), F32, "AK_xt2")
    xbf = k.sb((128, 8 * NT), BF16, "AK_xbf")
    outbf = k.sb((128, 8 * NT), BF16, "AK_obf")
    wk_bf = k.sb((128, 8 * D), BF16, "AK_wk")
    wv_bf = k.sb((128, 8 * D), BF16, "AK_wv")
    wstage = k.sb((128, D), F32, "AK_wst")
    bkc = k.sb((128, 8), F32, "AK_bk")
    bvrow = k.sb((1, D), F32, "AK_bvr")
    bvb = k.sb((128, D), F32, "AK_bvb")
    pool = [k.psum((128, 512), F32, f"AK_pp{i}") for i in range(4)]

    k.dma("sync", bkc[:, :], bk[:, :])
    k.dma("sync", bvrow[:, :], bv[:, :])
    for half in range(2):
        k.mm(ln.ps_b[:, :NT], ln.ones_row_f[:, :], bvrow[:, half * 512:(half + 1) * 512],
             start=True, stop=True)
        k.copy(bvb[:, half * 512:(half + 1) * 512], ln.ps_b[:, :NT], eng="scalar")
    for (wt, dst) in ((wkT, wk_bf), (wvT, wv_bf)):
        for kc in range(8):
            k.dma("sync", wstage[:, :], wt[kc * 128:(kc + 1) * 128, :])
            k.copy(dst[:, kc * D:(kc + 1) * D], wstage[:, :], eng="vector")

    for t in range(TOK // NT):
        tok = slice(t * NT, (t + 1) * NT)
        xs = [xstage[:, c * NT:(c + 1) * NT] for c in range(8)]
        xbs = [xbf[:, c * NT:(c + 1) * NT] for c in range(8)]
        for c in range(8):
            rows = slice(c * 128, (c + 1) * 128)
            k.dma("sync", xs[c], xbase[rows, tok])
            for i in range(4):
                k.dma("gpsimd", xtile2[:, c * NT:(c + 1) * NT], parts[i][rows, tok])
                k.tt(xs[c], xs[c], xtile2[:, c * NT:(c + 1) * NT], ALU.add)
            k.dma("scalar", x1own[rows, tok], xs[c])
            k.copy(xbs[c], xs[c], eng="vector")
        ln.stats(k, xs, xbs, NT)
        for c in range(8):
            ln.apply(k, xbs[c], xs[c], NT)
            k.dma("gpsimd", xlnown[c * 128:(c + 1) * 128, tok], xbs[c])
        for oc in range(8):
            p = pool[oc % 2]
            for kc in range(8):
                k.mm(p[:, :NT], wk_bf[:, kc * D + oc * 128:kc * D + (oc + 1) * 128],
                     xbs[kc], start=(kc == 0), stop=(kc == 7))
            k.tsc(xtile2[:, oc * NT:(oc + 1) * NT], p[:, :NT], bkc[:, oc:oc + 1], ALU.add)
            k.copy(outbf[:, oc * NT:(oc + 1) * NT], xtile2[:, oc * NT:(oc + 1) * NT],
                   eng="vector")
            k.dma("gpsimd", kown[oc * 128:(oc + 1) * 128, tok],
                  outbf[:, oc * NT:(oc + 1) * NT])
        for sub in range(4):
            tsub = slice(t * NT + sub * 128, t * NT + (sub + 1) * 128)
            for oh in range(2):
                p = pool[2 + oh]
                for kc in range(8):
                    k.mm(p[:, :NT],
                         xbf[:, kc * NT + sub * 128:kc * NT + (sub + 1) * 128],
                         wv_bf[:, kc * D + oh * 512:kc * D + (oh + 1) * 512],
                         start=(kc == 0), stop=(kc == 7))
                k.tt(xtile2[:, oh * NT:(oh + 1) * NT], p[:, :NT],
                     bvb[:, oh * 512:(oh + 1) * 512], ALU.add)
                k.copy(outbf[:, oh * NT:(oh + 1) * NT], xtile2[:, oh * NT:(oh + 1) * NT],
                       eng="vector")
            k.dma("gpsimd", vown[tsub, :], outbf[:, 0:1024])
    return k.finalize()


def build_AQ():
    k = KB("AQ")
    xlnown = k.din("xlnown", (D, TOK), BF16)
    x1own = k.din("x1own", (D, TOK), F32)
    kfull = k.din("kfull", (D, S), BF16)
    vfull = k.din("vfull", (S, D), BF16)
    wqT = k.din("wqT", (D, D), F32)
    woT = k.din("woT", (D, D), F32)
    bq = k.din("bq", (128, 8), F32)
    bo = k.din("bo", (128, 8), F32)
    x2own = k.dout("x2own", (D, TOK), F32)

    NT = 512
    GQ = 256
    q_sb = k.sb((128, 8 * TOK), BF16, "AQ_q")
    wq_bf = k.sb((128, 8 * D), BF16, "AQ_wq")
    wo_bf = k.sb((128, 8 * D), BF16, "AQ_wo")
    wstage = k.sb((128, D), F32, "AQ_wst")
    xbf = k.sb((128, 8 * NT), BF16, "AQ_xbf")
    bqc = k.sb((128, 8), F32, "AQ_bq")
    boc = k.sb((128, 8), F32, "AQ_bo")
    scores = k.sb((128, S), F32, "AQ_sc")
    cand = k.sb((128, 512), F32, "AQ_cand")
    cand2 = k.sb((128, 512), F32, "AQ_cand2")
    t32 = k.sb((128, 32), F32, "AQ_t32")
    colbuf = k.sb((128, 8), F32, "AQ_cols")
    probs = k.sb((128, S), BF16, "AQ_probs")
    probsT = k.sb((128, 64 * GQ), BF16, "AQ_probsT")
    attn_bf = k.sb((128, 8 * GQ), BF16, "AQ_attn")
    ktile = k.sb((128, 8 * NT), BF16, "AQ_kt")
    vtile = k.sb((128, D), BF16, "AQ_vt")
    ps_t = k.psum((128, 512), BF16, "AQ_pt")
    pool = [k.psum((128, 512), F32, f"AQ_pp{i}") for i in range(4)]
    otile = k.sb((128, 2 * GQ), F32, "AQ_ot")
    xres = k.sb((128, GQ), F32, "AQ_xres")

    k.dma("sync", bqc[:, :], bq[:, :])
    k.dma("sync", boc[:, :], bo[:, :])
    for (wt, dst) in ((wqT, wq_bf), (woT, wo_bf)):
        for kc in range(8):
            k.dma("sync", wstage[:, :], wt[kc * 128:(kc + 1) * 128, :])
            k.copy(dst[:, kc * D:(kc + 1) * D], wstage[:, :], eng="vector")
    for t in range(TOK // NT):
        tok = slice(t * NT, (t + 1) * NT)
        for c in range(8):
            k.dma("sync", xbf[:, c * NT:(c + 1) * NT],
                  xlnown[c * 128:(c + 1) * 128, tok])
        for oc in range(8):
            p = pool[oc % 4]
            for kc in range(8):
                k.mm(p[:, :NT], wq_bf[:, kc * D + oc * 128:kc * D + (oc + 1) * 128],
                     xbf[:, kc * NT:(kc + 1) * NT], start=(kc == 0), stop=(kc == 7))
            k.tsc(q_sb[:, oc * TOK + t * NT: oc * TOK + (t + 1) * NT], p[:, :NT],
                  bqc[:, oc:oc + 1], ALU.add)
    for gq in range(TOK // GQ):
        for qt2 in range(GQ // 128):
            qt = gq * 2 + qt2
            qrows = slice(qt * 128, (qt + 1) * 128)
            for kt in range(S // NT):
                ksl = slice(kt * NT, (kt + 1) * NT)
                for c in range(8):
                    k.dma("sync", ktile[:, c * NT:(c + 1) * NT],
                          kfull[c * 128:(c + 1) * 128, ksl])
                p = pool[kt % 2]
                for kc in range(8):
                    k.mm(p[:, :NT],
                         q_sb[:, kc * TOK + qt * 128: kc * TOK + (qt + 1) * 128],
                         ktile[:, kc * NT:(kc + 1) * NT],
                         start=(kc == 0), stop=(kc == 7))
                k.act(scores[:, ksl], p[:, :NT], AF.Copy, scale=1.0 / 32.0)
            for kc64 in range(64):
                k.op("vector", "max", reads=(scores[:, kc64 * 128:(kc64 + 1) * 128],),
                     writes=(cand[:, kc64 * 8:(kc64 + 1) * 8],),
                     out=cand[:, kc64 * 8:(kc64 + 1) * 8],
                     in_=scores[:, kc64 * 128:(kc64 + 1) * 128])
            cur = cand
            other = cand2
            for r in range(4):
                k.op("vector", "max", reads=(cur[:, :],),
                     writes=(t32[:, r * 8:(r + 1) * 8],),
                     out=t32[:, r * 8:(r + 1) * 8], in_=cur[:, :])
                if r < 3:
                    k.op("vector", "match_replace",
                         reads=(t32[:, r * 8:(r + 1) * 8], cur[:, :]),
                         writes=(other[:, :],),
                         out=other[:, :], in_to_replace=t32[:, r * 8:(r + 1) * 8],
                         in_values=cur[:, :], imm_value=-1e30)
                    cur, other = other, cur
            negm = colbuf[:, 0:1]
            texp = colbuf[:, 1:2]
            den = colbuf[:, 2:3]
            recip = colbuf[:, 3:4]
            k.tsc(negm, t32[:, 0:1], -1.0, ALU.mult)
            k.tt(texp, t32[:, 31:32], t32[:, 0:1], ALU.subtract)
            k.act(texp, texp, AF.Exp)
            k.act(scores[:, :], scores[:, :], AF.Exp, bias=negm)
            k.stt(probs[:, :], scores[:, :], texp, scores[:, :], ALU.is_ge, ALU.mult,
                  accum_out=den)
            k.op("vector", "reciprocal", reads=(den,), writes=(recip,),
                 out=recip, in_=den)
            k.tsc(probs[:, :], probs[:, :], recip, ALU.mult)
            for kc64 in range(64):
                k.op("tensor", "transpose", reads=(probs[:, kc64 * 128:(kc64 + 1) * 128],),
                     writes=(ps_t,),
                     out=ps_t[:, qt2 * 128:(qt2 + 1) * 128],
                     in_=probs[:, kc64 * 128:(kc64 + 1) * 128],
                     identity=ident_bf(k))
                k.copy(probsT[:, kc64 * GQ + qt2 * 128: kc64 * GQ + (qt2 + 1) * 128],
                       ps_t[:, qt2 * 128:(qt2 + 1) * 128], eng="scalar")
        for half in range(2):
            for kc64 in range(64):
                k.dma("sync", vtile[:, :], vfull[kc64 * 128:(kc64 + 1) * 128, :])
                for ec in range(4):
                    e0 = half * 4 + ec
                    k.mm(pool[ec][:, :GQ], vtile[:, e0 * 128:(e0 + 1) * 128],
                         probsT[:, kc64 * GQ:(kc64 + 1) * GQ],
                         start=(kc64 == 0), stop=(kc64 == 63))
            for ec in range(4):
                e0 = half * 4 + ec
                k.copy(attn_bf[:, e0 * GQ:(e0 + 1) * GQ], pool[ec][:, :GQ], eng="vector")
        for oc in range(8):
            p = pool[oc % 4]
            for ec in range(8):
                k.mm(p[:, :GQ], wo_bf[:, ec * D + oc * 128:ec * D + (oc + 1) * 128],
                     attn_bf[:, ec * GQ:(ec + 1) * GQ], start=(ec == 0), stop=(ec == 7))
            k.dma("sync", xres[:, :], x1own[oc * 128:(oc + 1) * 128,
                                            gq * GQ:(gq + 1) * GQ])
            k.tt(otile[:, (oc % 2) * GQ:(oc % 2) * GQ + GQ], p[:, :GQ], xres[:, :],
                 ALU.add)
            k.tsc(otile[:, (oc % 2) * GQ:(oc % 2) * GQ + GQ],
                  otile[:, (oc % 2) * GQ:(oc % 2) * GQ + GQ], boc[:, oc:oc + 1], ALU.add)
            k.dma("gpsimd", x2own[oc * 128:(oc + 1) * 128, gq * GQ:(gq + 1) * GQ],
                  otile[:, (oc % 2) * GQ:(oc % 2) * GQ + GQ])
    return k.finalize()


_IDENT = {}


def ident_bf(k):
    if id(k) not in _IDENT:
        buf = k.sb((128, 128), BF16, "identbf")
        idram = k.din("ident_in", (128, 128), F32)
        st = k.sb((128, 128), F32, "identst")
        k.dma("sync", st[:, :], idram[:, :])
        k.copy(buf[:, :], st[:, :], eng="vector")
        _IDENT[id(k)] = buf[:, :]
    return _IDENT[id(k)]


def stage_AK(xbase_fm, parts, W):
    maps = []
    for c in range(NCORES):
        b, qc = c // 4, c % 4
        tok = slice(qc * TOK, (qc + 1) * TOK)
        m = {
            "xbase": xbase_fm[b][:, tok],
            "wkT": np.ascontiguousarray(np.asarray(W["L0_attn_Wk"]).T),
            "wvT": np.ascontiguousarray(np.asarray(W["L0_attn_Wv"]).T),
            "bk": np.asarray(W["L0_attn_bk"]).reshape(8, 128).T,
            "bv": np.asarray(W["L0_attn_bv"]).reshape(1, D),
        }
        for i in range(4):
            m[f"p{i}"] = parts[b][i][:, tok]
        maps.append({kk: np.ascontiguousarray(vv, dtype=np.float32) for kk, vv in m.items()})
    return maps


def run_A(xbase_fm, parts, W):
    res = _run("AK", build_AK, stage_AK(xbase_fm, parts, W))
    x1 = [np.concatenate([res[b * 4 + q]["x1own"] for q in range(4)], axis=1)
          for b in range(2)]
    xln = [np.concatenate([res[b * 4 + q]["xlnown"] for q in range(4)], axis=1)
           for b in range(2)]
    kf = [np.concatenate([res[b * 4 + q]["kown"] for q in range(4)], axis=1)
          for b in range(2)]
    vf = [np.concatenate([res[b * 4 + q]["vown"] for q in range(4)], axis=0)
          for b in range(2)]
    maps = []
    ident = np.eye(128, dtype=np.float32)
    for c in range(NCORES):
        b, qc = c // 4, c % 4
        tok = slice(qc * TOK, (qc + 1) * TOK)
        m = {
            "xlnown": xln[b][:, tok],
            "x1own": np.ascontiguousarray(x1[b][:, tok]),
            "kfull": kf[b],
            "vfull": vf[b],
            "wqT": np.ascontiguousarray(np.asarray(W["L0_attn_Wq"]).T.astype(np.float32)),
            "woT": np.ascontiguousarray(np.asarray(W["L0_attn_Wo"]).T.astype(np.float32)),
            "bq": np.ascontiguousarray(np.asarray(W["L0_attn_bq"]).reshape(8, 128).T.astype(np.float32)),
            "bo": np.ascontiguousarray(np.asarray(W["L0_attn_bo"]).reshape(8, 128).T.astype(np.float32)),
            "ident_in": ident,
        }
        maps.append(m)
    res2 = _run("AQ", build_AQ, maps)
    x2 = [np.concatenate([res2[b * 4 + q]["x2own"] for q in range(4)], axis=1)
          for b in range(2)]
    return x2, x1


EXT = 2560
EXTP = EXT + 1


def build_LT_entry():
    return _build_LT("entry")


def build_LT_mid():
    return _build_LT("mid")


def build_LT_final():
    return _build_LT("final")


def _build_LT(mode):
    k = KB("LT_" + mode)
    NT = 512
    NTIL = EXT // NT

    if mode == "entry":
        xbase = k.din("xbase", (D, EXT), F32)
        parts = [k.din(f"p{i}", (D, EXT), F32) for i in range(4)]
        residout = k.dout("residout", (D, TOK), F32)
        xsrc = k.dtmp("LT_x0", (D, EXT), F32)
    else:
        xsrc = k.din("xext", (D, EXT), F32)
    if mode == "final":
        resid_own = k.din("resid_own", (D, TOK), F32)
        xout = k.dout("xfin", (D, TOK), F32)
    else:
        xout = k.dout("xout", (D, TOK), F32)
    wqkv = k.din("wqkv", (D, 3 * D), F32)
    wo = k.din("wo", (D, D), F32)
    w1p = k.din("w1p", (D, 5632), F32)
    w2p = k.din("w2p", (2816, D), F32)
    b1a = k.din("b1a", (128, 22), F32)
    b1g = k.din("b1g", (128, 22), F32)
    b2c = k.din("b2c", (128, 8), F32)
    lng = k.din("lng", (128, 8), F32)
    lnb = k.din("lnb", (128, 8), F32)
    qscol = k.din("qscol", (128, 1), F32)
    kscol = k.din("kscol", (128, 1), F32)
    qsrow = k.din("qsrow", (1, 64), F32)
    ksrow = k.din("ksrow", (1, 64), F32)
    tbl_cs = k.din("tbl_cs", (128, EXT), F32)
    tbl_ss = k.din("tbl_ss", (128, EXT), F32)
    tbl_ci = k.din("tbl_ci", (128, EXT), F32)
    tbl_si = k.din("tbl_si", (128, EXT), F32)
    rotPT = k.din("rotPT", (128, 128), F32)
    bdiag = k.din("bdiag", (128, 128), F32)
    identb = k.din("identb", (128, 128), BF16)
    masks = k.din("masks", (9, 512, 256), BF16)

    QST = EXT - 256
    xnew_dram = k.dtmp("LT_xnew", (D, EXT), F32)
    v_dram = k.dtmp("LT_v", (EXT, D), BF16)
    u_dram = k.dtmp("LT_u", (2816, TOK), BF16)
    q_dram = k.dtmp("LT_qd", (D, QST), F32)
    k_dram = k.dtmp("LT_kd", (D, EXT), F32)

    x_bf = k.sb((128, 8 * EXTP), BF16, "LT_x")
    u_bf = k.sb((128, 22 * 512), BF16, "LT_ub")
    w2_bf = k.sb((128, 22 * 1024), BF16, "LT_w2")
    attn_bf = k.sb((128, 8 * EXT), BF16, "LT_ab")
    tb0 = k.sb((128, 512), F32, "LT_tb0")
    tb1 = k.sb((128, 512), F32, "LT_tb1")
    rotP_sb = k.sb((128, 128), F32, "LT_rp")
    bd_sb = k.sb((128, 128), F32, "LT_bd")
    id_sb = k.sb((128, 128), BF16, "LT_id")
    qtile = k.sb((64, 2 * 256), F32, "LT_qt")
    ktile = k.sb((64, 2 * 512), F32, "LT_kt")
    qraw = k.sb((128, 2 * 512), F32, "LT_qraw")
    ta2 = k.sb((128, 2 * 512), F32, "LT_ta2")
    us2 = k.sb((128, 2 * 512), F32, "LT_us2")
    rn2 = k.sb((128, 2 * 512), F32, "LT_rn2")
    wstage = k.sb((128, 512), F32, "LT_wst")
    wcol = k.sb((128, 4096), BF16, "LT_wc")
    xf1 = k.sb((128, 512), F32, "LT_xf1")
    cols = k.sb((128, 64), F32, "LT_cols")
    rowbuf = k.sb((1, 128), F32, "LT_row")
    mqr = k.sb((1, 4), F32, "LT_mq")

    recb = k.sb((64, 2 * 256), F32, "LT_rcb")
    mask_sb = k.sb((128, 4 * 256), BF16, "LT_msk")

    probsT = k.sb((128, 2 * 4 * 256), BF16, "LT_pT")
    sq_bf = k.sb((128, 512), BF16, "LT_sq")
    gelu_t = k.sb((128, 512), BF16, "LT_gel")

    pool = [k.psum((128, 512), F32, f"LT_pp{i}") for i in range(4)]
    ln = LNHelper(k, maxN=NT, name="LTln", ps_b=pool[3], tmp=ta2[:, 0:512])
    recip_sb = ln.row[0:1, 1024:1536]
    ps_bank = k.psum((128, 512), F32, "LT_bank")
    ps_bank2 = k.psum((128, 512), F32, "LT_bank2")
    ps_sc_p = [ps_bank[:, 0:256], ps_bank2[:, 0:256]]
    ps_ao_p = [ps_bank[0:64, 256:512], ps_bank2[0:64, 256:512]]
    ps_den_p = [ln.ps_s[0:1, 256:512], ln.ps_q[0:1, 256:512]]

    qs_c = cols[:, 0:1]
    ks_c = cols[:, 1:2]
    negB = cols[:, 2:3]
    b1a_c = cols[:, 8:30]
    b1g_c = cols[:, 30:52]
    b2_c = cols[:, 52:60]
    lng_c = cols[:, 60:61]
    lngc = k.sb((128, 8), F32, "LT_lng2")
    lnbc = k.sb((128, 8), F32, "LT_lnb2")

    k.dma("sync", qs_c, qscol[:, :])
    k.dma("sync", ks_c, kscol[:, :])
    k.dma("sync", cols[:, 8:30], b1a[:, :])
    k.dma("sync", cols[:, 30:52], b1g[:, :])
    k.dma("sync", cols[:, 52:60], b2c[:, :])
    k.dma("sync", lngc[:, :], lng[:, :])
    k.dma("sync", lnbc[:, :], lnb[:, :])
    k.dma("sync", rotP_sb[:, :], rotPT[:, :])
    k.dma("sync", bd_sb[:, :], bdiag[:, :])
    k.dma("sync", id_sb[:, :], identb[:, :])
    k.dma("sync", rowbuf[:, 0:64], qsrow[:, :])
    k.dma("sync", rowbuf[:, 64:128], ksrow[:, :])

    k.op("vector", "tensor_reduce", reads=(rowbuf[:, 0:64],), writes=(mqr[:, 0:1],),
         out=mqr[:, 0:1], in_=rowbuf[:, 0:64], axis=AX.X, op=ALU.max,
         apply_absolute_value=True)
    k.op("vector", "tensor_reduce", reads=(rowbuf[:, 64:128],), writes=(mqr[:, 1:2],),
         out=mqr[:, 1:2], in_=rowbuf[:, 64:128], axis=AX.X, op=ALU.max,
         apply_absolute_value=True)
    k.tt(mqr[:, 2:3], mqr[:, 0:1], mqr[:, 1:2], ALU.mult)
    k.tsc(mqr[:, 3:4], mqr[:, 2:3], -8.0, ALU.mult)
    k.mm(pool[0][:, 0:1], ln.ones_row_f[:, :], mqr[:, 3:4], start=True, stop=True)
    k.copy(negB, pool[0][:, 0:1], eng="scalar")

    xf2 = qraw[:, 512:1024]

    def xsh(c, off, n):
        base = c * EXTP + 1
        if c >= 4:
            return x_bf[:, base + off - 1: base + off - 1 + n]
        return x_bf[:, base + off: base + off + n]

    def xv(c, off, n):
        base = c * EXTP + 1
        return x_bf[:, base + off: base + off + n]

    for c in range(8):
        k.memset(x_bf[:, c * EXTP: c * EXTP + 1], 0.0)
        k.memset(attn_bf[:, c * EXT: c * EXT + 256], 0.0)
    for t in range(NTIL):
        tok = slice(t * NT, (t + 1) * NT)
        if mode == "entry":
            for c in range(8):
                rows = slice(c * 128, (c + 1) * 128)
                k.dma("sync", xf1[:, :], xbase[rows, tok])
                for i in range(4):
                    k.dma("gpsimd", xf2[:, :], parts[i][rows, tok])
                    k.tt(xf1[:, :], xf1[:, :], xf2[:, :], ALU.add)
                k.dma("scalar", xsrc[rows, tok], xf1[:, :])
                if t >= 1:
                    k.dma("scalar", residout[rows, (t - 1) * NT: t * NT], xf1[:, :])
            for c in range(8):
                rows = slice(c * 128, (c + 1) * 128)
                k.dma("sync", xf1[:, :], xsrc[rows, tok])
                k.copy(sq_bf[:, :NT], xf1[:, :], eng="vector")
                k.copy(xv(c, t * NT, NT), sq_bf[:, :NT], eng="vector")
                ln.stats_chunk(k, c, 8, xf1[:, :], xv(c, t * NT, NT), NT)
            ln.finish_stats(k, NT)
            for c in range(8):
                rows = slice(c * 128, (c + 1) * 128)
                k.dma("sync", xf1[:, :], xsrc[rows, tok])
                ln.apply(k, xv(c, t * NT, NT), xf1[:, :], NT)
                k.tt(xf2[:, :], xf1[:, :], ln.mu_b[:, :NT], ALU.subtract)
                k.tt(xf2[:, :], xf2[:, :], ln.rstd_b[:, :NT], ALU.mult)
                k.dma("scalar", xnew_dram[rows, tok], xf2[:, :])
        else:
            for c in range(8):
                rows = slice(c * 128, (c + 1) * 128)
                k.dma("sync", xf1[:, :], xsrc[rows, tok])
                k.copy(sq_bf[:, :NT], xf1[:, :], eng="vector")
                k.copy(xv(c, t * NT, NT), sq_bf[:, :NT], eng="vector")
    res_dram = xnew_dram if mode == "entry" else xsrc

    eps24 = 1e-24
    for qk in range(2):
        scol = qs_c if qk == 0 else ks_c
        tsrc0 = tbl_cs if qk == 0 else tbl_ci
        tsrc1 = tbl_ss if qk == 0 else tbl_si
        for oc in range(8):
            m0 = qk * D + oc * 128
            for kc in range(8):
                k.dma("sync", wstage[:, 0:128], wqkv[kc * 128:(kc + 1) * 128, m0:m0 + 128])
                k.copy(wcol[:, kc * 128:(kc + 1) * 128], wstage[:, 0:128], eng="vector")
            for t in range(NTIL):
                par = t % 2
                p = pool[par]
                qr = qraw[:, par * 512: par * 512 + NT]
                tav = ta2[:, par * 512: par * 512 + NT]
                usv = us2[:, par * 512: par * 512 + NT]
                rnv = rn2[:, par * 512: par * 512 + NT]
                psb = pool[2][:, par * 256:(par + 1) * 256] if False else pool[2 + par]
                for kc in range(8):
                    k.mm(p[:, :NT], wcol[:, kc * 128:(kc + 1) * 128],
                         xsh(kc, t * NT, NT), start=(kc == 0), stop=(kc == 7))
                k.copy(qr, p[:, :NT], eng="scalar")
                k.dma("sync", tb0[:, :], tsrc0[:, t * NT:(t + 1) * NT])
                k.dma("sync", tb1[:, :], tsrc1[:, t * NT:(t + 1) * NT])
                k.act(tav, qr, AF.Square)
                k.mm(psb[:, :NT], bd_sb[:, :], tav, start=True, stop=True)
                k.act(rnv, psb[:, :NT], AF.Ln, bias=eps24)
                k.act(rnv, rnv, AF.Exp, scale=-0.5)
                k.mm(psb[:, :NT], rotP_sb[:, :], qr, start=True, stop=True)
                k.stt(usv, qr, scol, tb0[:, :NT], ALU.mult, ALU.mult)
                k.stt(tav, psb[:, :NT], scol, tb1[:, :NT], ALU.mult, ALU.mult)
                k.tt(tav, tav, usv, ALU.add)
                k.tt(qr, tav, rnv, ALU.mult)
                rows = slice(oc * 128, (oc + 1) * 128)
                if qk == 1:
                    k.dma("scalar", k_dram[rows, t * NT:(t + 1) * NT], qr)
                elif t == 0:
                    k.dma("scalar", q_dram[rows, 0:256], qr[:, 256:NT])
                else:
                    k.dma("scalar", q_dram[rows, t * NT - 256:(t + 1) * NT - 256], qr)

    for vc2 in range(2):
        for kc in range(8):
            k.dma("sync", wstage[:, :], wqkv[kc * 128:(kc + 1) * 128,
                                             2 * D + vc2 * 512: 2 * D + (vc2 + 1) * 512])
            k.copy(wcol[:, kc * 512:(kc + 1) * 512], wstage[:, :], eng="vector")
        for t in range(NTIL):
            for sub in range(4):
                p = pool[2 + sub % 2]
                for kc in range(8):
                    k.mm(p[:, :NT], xsh(kc, t * NT + sub * 128, 128),
                         wcol[:, kc * 512:(kc + 1) * 512], start=(kc == 0), stop=(kc == 7))
                k.copy(sq_bf[:, :NT], p[:, :NT], eng="vector")
                k.dma("scalar",
                      v_dram[t * NT + sub * 128: t * NT + (sub + 1) * 128,
                             vc2 * 512:(vc2 + 1) * 512], sq_bf[:, :NT])

    for w in range(1, 10):
        kst = w * 256 - 256
        vtile = wcol
        for kt in range(4):
            k.dma("sync", mask_sb[:, kt * 256:(kt + 1) * 256],
                  masks[w - 1, kt * 128:(kt + 1) * 128, :])
            k.dma("sync", vtile[:, kt * 1024:(kt + 1) * 1024],
                  v_dram[kst + kt * 128: kst + (kt + 1) * 128, :])
        for h in range(16):
            c, hh = h // 2, h % 2
            par = h % 2
            ps_sc = ps_sc_p[par]
            ps_ao = ps_ao_p[par]
            ps_den = ps_den_p[par]
            pT = probsT[:, par * 1024:(par + 1) * 1024]
            qt_ = qtile[:, par * 256:(par + 1) * 256]
            kt_ = ktile[:, par * 512:(par + 1) * 512]
            rc_ = recip_sb[0:1, par * 256:(par + 1) * 256]
            rb_ = recb[:, par * 256:(par + 1) * 256]
            prow = slice(hh * 64, (hh + 1) * 64)
            hrows = slice(c * 128 + hh * 64, c * 128 + (hh + 1) * 64)
            k.dma("sync", qt_, q_dram[hrows, (w - 1) * 256: w * 256])
            for kt in range(4):
                k.dma("sync", kt_[:, kt * 128:(kt + 1) * 128],
                      k_dram[hrows, kst + kt * 128: kst + (kt + 1) * 128])
            for kt in range(4):
                k.mm(ps_sc, kt_[:, kt * 128:(kt + 1) * 128], qt_, start=True, stop=False)
                k.mm(ps_sc, id_sb[:, :], mask_sb[:, kt * 256:(kt + 1) * 256],
                     start=False, stop=True)
                k.act(pT[:, kt * 256:(kt + 1) * 256], ps_sc, AF.Exp,
                      scale=8.0, bias=negB)
            for kt in range(4):
                k.mm(ps_den, ln.ones_col_bf[:, :], pT[:, kt * 256:(kt + 1) * 256],
                     start=(kt == 0), stop=(kt == 3))
            k.op("vector", "reciprocal", reads=(ps_den,), writes=(rc_,),
                 out=rc_, in_=ps_den)
            for kt in range(4):
                k.mm(ps_ao, vtile[:, kt * 1024 + h * 64: kt * 1024 + (h + 1) * 64],
                     pT[:, kt * 256:(kt + 1) * 256], start=(kt == 0), stop=(kt == 3))
            k.mm(ps_sc[0:64, :], ln.ones_row_f[:, 0:64], rc_, start=True, stop=True)
            k.copy(rb_, ps_sc[0:64, :], eng="scalar")
            k.tt(attn_bf[prow, c * EXT + w * 256: c * EXT + (w + 1) * 256],
                 ps_ao, rb_, ALU.mult)

    for oc in range(8):
        rows = slice(oc * 128, (oc + 1) * 128)
        for ec in range(8):
            k.dma("sync", wstage[:, 0:128], wo[ec * 128:(ec + 1) * 128, oc * 128:(oc + 1) * 128])
            k.copy(wcol[:, ec * 128:(ec + 1) * 128], wstage[:, 0:128], eng="vector")
        for t in range(NTIL):
            p = pool[t % 2]
            for ec in range(8):
                k.mm(p[:, :NT], wcol[:, ec * 128:(ec + 1) * 128],
                     attn_bf[:, ec * EXT + t * NT: ec * EXT + (t + 1) * NT],
                     start=(ec == 0), stop=(ec == 7))
            k.dma("sync", xf1[:, :], res_dram[rows, t * NT:(t + 1) * NT])
            k.tt(xf2[:, :], p[:, :NT], xf1[:, :], ALU.add)
            k.dma("scalar", xnew_dram[rows, t * NT:(t + 1) * NT], xf2[:, :])
            k.copy(xv(oc, t * NT, NT), xf2[:, :], eng="vector")

    for t in range(1, NTIL):
        for c in range(8):
            rows = slice(c * 128, (c + 1) * 128)
            off = t * NT - 1 if c >= 4 else t * NT
            k.dma("sync", xf1[:, :], xnew_dram[rows, off: off + NT])
            ln.stats_chunk(k, c, 8, xf1[:, :], xsh(c, t * NT, NT), NT)
        ln.finish_stats(k, NT)
        for c in range(8):
            rows = slice(c * 128, (c + 1) * 128)
            off = t * NT - 1 if c >= 4 else t * NT
            k.dma("sync", xf1[:, :], xnew_dram[rows, off: off + NT])
            ln.apply(k, attn_bf[:, c * EXT + (t - 1) * NT: c * EXT + t * NT],
                     xf1[:, :], NT, gcol=lngc[:, c:c + 1],
                     bcol=lnbc[:, c:c + 1].to_broadcast([128, NT]))
    for j in range(22):
        for kc in range(8):
            k.dma("sync", wstage[:, 0:128], w1p[kc * 128:(kc + 1) * 128,
                                                j * 128:(j + 1) * 128])
            k.copy(wcol[:, kc * 128:(kc + 1) * 128], wstage[:, 0:128], eng="vector")
            k.dma("sync", wstage[:, 128:256], w1p[kc * 128:(kc + 1) * 128,
                                                  2816 + j * 128: 2816 + (j + 1) * 128])
            k.copy(wcol[:, 1024 + kc * 128: 1024 + (kc + 1) * 128], wstage[:, 128:256],
                   eng="vector")
        for t4 in range(4):
            xnsl = [attn_bf[:, c * EXT + t4 * NT: c * EXT + (t4 + 1) * NT] for c in range(8)]
            for kc in range(8):
                k.mm(pool[0][:, :NT], wcol[:, kc * 128:(kc + 1) * 128], xnsl[kc],
                     start=(kc == 0), stop=(kc == 7))
            for kc in range(8):
                k.mm(pool[1][:, :NT], wcol[:, 1024 + kc * 128: 1024 + (kc + 1) * 128],
                     xnsl[kc], start=(kc == 0), stop=(kc == 7))
            k.tsc(ta2[:, :NT], pool[0][:, :NT], b1a_c[:, j:j + 1], ALU.add)
            k.act(gelu_t[:, :NT], pool[1][:, :NT], AF.Gelu, bias=b1g_c[:, j:j + 1])
            k.tt(sq_bf[:, :NT], ta2[:, :NT], gelu_t[:, :NT], ALU.mult)
            k.dma("scalar", u_dram[j * 128:(j + 1) * 128, t4 * NT:(t4 + 1) * NT],
                  sq_bf[:, :NT])
    for jc in range(22):
        for oc in range(8):
            k.dma("sync", wstage[:, 0:128], w2p[jc * 128:(jc + 1) * 128,
                                                oc * 128:(oc + 1) * 128])
            k.copy(w2_bf[:, jc * 1024 + oc * 128: jc * 1024 + (oc + 1) * 128],
                   wstage[:, 0:128], eng="vector")
    for t4 in range(4):
        for jc in range(22):
            k.dma("sync", u_bf[:, jc * NT:(jc + 1) * NT],
                  u_dram[jc * 128:(jc + 1) * 128, t4 * NT:(t4 + 1) * NT])
        for oc in range(8):
            rows = slice(oc * 128, (oc + 1) * 128)
            p = pool[oc % 2]
            for jc in range(22):
                k.mm(p[:, :NT], w2_bf[:, jc * 1024 + oc * 128: jc * 1024 + (oc + 1) * 128],
                     u_bf[:, jc * NT:(jc + 1) * NT], start=(jc == 0), stop=(jc == 21))
            k.tsc(ta2[:, 512:512 + NT], p[:, :NT], b2_c[:, oc:oc + 1], ALU.add)
            k.dma("sync", xf1[:, :], xnew_dram[rows, NT + t4 * NT: NT + (t4 + 1) * NT])
            k.tt(xf2[:, :], ta2[:, 512:512 + NT], xf1[:, :], ALU.add)
            if mode == "final":
                k.dma("sync", xf1[:, :], resid_own[rows, t4 * NT:(t4 + 1) * NT])
                k.stt(xf2[:, :], xf2[:, :], 0.5, xf1[:, :], ALU.mult, ALU.add)
            k.dma("scalar", xout[rows, t4 * NT:(t4 + 1) * NT], xf2[:, :])
    return k.finalize()


import ml_dtypes

BF = ml_dtypes.bfloat16


def _ext_slice(full_fm, s):
    lo = s - 512
    out = np.zeros((D, EXT), dtype=np.float32)
    a = max(0, lo)
    out[:, a - lo:] = full_fm[:, a: s + TOK]
    return out


def _lt_tables(s):
    g = np.arange(s - 512, s + TOK, dtype=np.float64)
    half = np.arange(0, 64, 2, dtype=np.float64)
    inv_freq = 1.0 / 10000.0 ** (half / 64.0)
    fr = g[:, None] * inv_freq[None, :]
    fr = np.concatenate([fr, fr], axis=1)
    base_scale = (half + 0.4 * 64) / (1.4 * 64)
    power = (g - 8192 // 2) / (256 // 2)
    scale = base_scale[None, :] ** power[:, None]
    scale = np.concatenate([scale, scale], axis=1)
    cs = np.cos(fr) * scale
    ss = np.sin(fr) * scale
    ci = np.cos(fr) / scale
    si = np.sin(fr) / scale

    def pack(v):
        vt = v.T.astype(np.float32)
        return np.ascontiguousarray(np.vstack([vt, vt]))
    return pack(cs), pack(ss), pack(ci), pack(si)


def _lt_masks(s):
    M = np.zeros((9, 512, 256), dtype=np.float32)
    for w in range(1, 10):
        gws = s - 512 + w * 256
        if gws < 0:
            continue
        it = gws + np.arange(256)[None, :]
        jl = np.arange(512)[:, None]
        jb = np.where(jl < 256,
                      np.where(gws - 256 + jl >= 0, gws - 256 + jl, -1),
                      gws + (jl - 256))
        keep = (jb != -1) & (it >= jb) & (it <= jb + 256)
        M[w - 1] = np.where(keep, 0.0, -30000.0)
    return M.astype(BF)


def _rot_consts():
    P = np.zeros((128, 128), dtype=np.float32)
    for blk in range(2):
        o = blk * 64
        for j in range(32):
            P[o + j, o + j + 32] = -1.0
        for j in range(32, 64):
            P[o + j, o + j - 32] = 1.0
    PT = np.ascontiguousarray(P.T)
    BD = np.zeros((128, 128), dtype=np.float32)
    BD[:64, :64] = 1.0
    BD[64:, 64:] = 1.0
    return PT, BD, np.eye(128, dtype=np.float32).astype(BF)


def _col8(v):
    return np.ascontiguousarray(np.asarray(v, np.float32).reshape(8, 128).T)


def _pad_w1(W1, b1):
    a, g = W1[:, :2730], W1[:, 2730:]
    w1p = np.zeros((D, 5632), np.float32)
    w1p[:, :2730] = a
    w1p[:, 2816:2816 + 2730] = g
    b1a = np.zeros(2816, np.float32)
    b1a[:2730] = b1[:2730]
    b1g = np.zeros(2816, np.float32)
    b1g[:2730] = b1[2730:]
    return (w1p, np.ascontiguousarray(b1a.reshape(22, 128).T),
            np.ascontiguousarray(b1g.reshape(22, 128).T))


def _pad_w2(W2):
    w2p = np.zeros((2816, D), np.float32)
    w2p[:2730, :] = W2
    return w2p


def stage_LT(mode, layer, W, xfull_fm=None, base_fm=None, parts_fm=None,
             resid_own=None):
    i = layer
    wqkv = np.ascontiguousarray(np.asarray(W["L1_ffn_Wqkv"][i], np.float32))
    wo = np.ascontiguousarray(np.asarray(W["L1_ffn_Wo"][i], np.float32))
    w1p, b1a, b1g = _pad_w1(np.asarray(W["L1_ffn_W1"][i], np.float32),
                            np.asarray(W["L1_ffn_b1"][i], np.float32))
    w2p = _pad_w2(np.asarray(W["L1_ffn_W2"][i], np.float32))
    b2c = _col8(W["L1_ffn_b2"][i])
    lng = _col8(W["L1_ffn_ln_g"][i])
    lnb = _col8(W["L1_ffn_ln_b"][i])
    qs = np.asarray(W["L1_ffn_q_scale"][i], np.float32)
    ks = np.asarray(W["L1_ffn_k_scale"][i], np.float32)
    PT, BD, ID = _rot_consts()
    maps = []
    for c in range(NCORES):
        b, qc = c // 4, c % 4
        s = qc * TOK
        cs, ss, ci, si = _lt_tables(s)
        m = {
            "wqkv": wqkv, "wo": wo, "w1p": w1p, "w2p": w2p,
            "b1a": b1a, "b1g": b1g, "b2c": b2c, "lng": lng, "lnb": lnb,
            "qscol": np.tile(qs, 2).reshape(128, 1),
            "kscol": np.tile(ks, 2).reshape(128, 1),
            "qsrow": qs.reshape(1, 64), "ksrow": ks.reshape(1, 64),
            "tbl_cs": cs, "tbl_ss": ss, "tbl_ci": ci, "tbl_si": si,
            "rotPT": PT, "bdiag": BD, "identb": ID,
            "masks": _lt_masks(s),
        }
        if mode == "entry":
            m["xbase"] = _ext_slice(base_fm[b], s)
            for j in range(4):
                m[f"p{j}"] = _ext_slice(parts_fm[b][j], s)
        else:
            m["xext"] = _ext_slice(xfull_fm[b], s)
        if mode == "final":
            m["resid_own"] = np.ascontiguousarray(resid_own[b][:, s:s + TOK])
        maps.append(m)
    return maps


_LT_BUILDERS = {"entry": build_LT_entry, "mid": build_LT_mid, "final": build_LT_final}


def run_LT_chain(base_fm, parts_fm, W):
    res = _run("LT_entry", build_LT_entry,
               stage_LT("entry", 0, W, base_fm=base_fm, parts_fm=parts_fm))
    resid = [np.concatenate([res[b * 4 + q]["residout"] for q in range(4)], axis=1)
             for b in range(2)]
    x = [np.concatenate([res[b * 4 + q]["xout"] for q in range(4)], axis=1)
         for b in range(2)]
    for layer in range(1, 5):
        res = _run("LT_mid", build_LT_mid, stage_LT("mid", layer, W, xfull_fm=x))
        x = [np.concatenate([res[b * 4 + q]["xout"] for q in range(4)], axis=1)
             for b in range(2)]
    res = _run("LT_final", build_LT_final,
               stage_LT("final", 5, W, xfull_fm=x, resid_own=resid))
    xn = [np.concatenate([res[b * 4 + q]["xfin"] for q in range(4)], axis=1)
          for b in range(2)]
    return xn, resid


def build_M1():
    k = KB("M1")
    x4 = k.din("x4full", (D, S), F32)
    ipw_x = k.din("ipw_x", (D, 512), F32)
    ipw_r = k.din("ipw_r", (D, 512), F32)
    convw = k.din("convw", (128, 16), F32)
    convb = k.din("convb", (128, 4), F32)
    xpw = k.din("xpw", (512, 96), F32)
    h_q = k.dout("h_q", (512, S), BF16)
    res_q = k.dout("res_q", (512, S), BF16)
    xdbl = k.dout("xdbl", (96, S), F32)

    xp_dram = k.dtmp("M1_xp", (512, S + 3), F32)

    NT = 512
    ln = LNHelper(k, maxN=NT, name="M1ln")
    xstage = k.sb((128, 8 * NT), F32, "M1_xst")
    xbf = k.sb((128, 8 * NT), BF16, "M1_xbf")
    wx_bf = k.sb((128, 8 * 512), BF16, "M1_wx")
    wr_bf = k.sb((128, 8 * 512), BF16, "M1_wr")
    xpw_bf = k.sb((128, 4 * 96), BF16, "M1_xpw")
    wstage = k.sb((128, 512), F32, "M1_wst")
    cw = k.sb((128, 16), F32, "M1_cw")
    cb = k.sb((128, 4), F32, "M1_cb")
    xf1 = k.sb((128, 512), F32, "M1_f1")
    hbf = k.sb((128, 4 * NT), BF16, "M1_h")
    conv_in = k.sb((128, NT + 3), F32, "M1_ci")
    acc = k.sb((128, NT), F32, "M1_acc")
    pool = [k.psum((128, 512), F32, f"M1_pp{i}") for i in range(4)]
    ps_xd = k.psum((96, 512), F32, "M1_pxd")

    k.dma("sync", cw[:, :], convw[:, :])
    k.dma("sync", cb[:, :], convb[:, :])
    for (wt, dst) in ((ipw_x, wx_bf), (ipw_r, wr_bf)):
        for kc in range(8):
            k.dma("sync", wstage[:, :], wt[kc * 128:(kc + 1) * 128, :])
            k.copy(dst[:, kc * 512:(kc + 1) * 512], wstage[:, :], eng="vector")
    for c4 in range(4):
        k.dma("sync", wstage[:, 0:96], xpw[c4 * 128:(c4 + 1) * 128, :])
        k.copy(xpw_bf[:, c4 * 96:(c4 + 1) * 96], wstage[:, 0:96], eng="vector")
    k.memset(xf1[:, 0:3], 0.0)
    for c4 in range(4):
        k.dma("scalar", xp_dram[c4 * 128:(c4 + 1) * 128, 0:3], xf1[:, 0:3])

    for t in range(S // NT):
        tok = slice(t * NT, (t + 1) * NT)
        xs = [xstage[:, c * NT:(c + 1) * NT] for c in range(8)]
        xbs = [xbf[:, c * NT:(c + 1) * NT] for c in range(8)]
        for c in range(8):
            k.dma("sync", xs[c], x4[c * 128:(c + 1) * 128, tok])
            k.copy(xbs[c], xs[c], eng="vector")
        ln.stats(k, xs, xbs, NT)
        for c in range(8):
            ln.apply(k, xbs[c], xs[c], NT)
        for c4 in range(4):
            p = pool[c4 % 2]
            for kc in range(8):
                k.mm(p[:, :NT], wx_bf[:, kc * 512 + c4 * 128: kc * 512 + (c4 + 1) * 128],
                     xbs[kc], start=(kc == 0), stop=(kc == 7))
            k.copy(xf1[:, :], p[:, :NT], eng="scalar")
            k.dma("scalar", xp_dram[c4 * 128:(c4 + 1) * 128, 3 + t * NT: 3 + (t + 1) * NT],
                  xf1[:, :])
            pr = pool[2 + c4 % 2]
            for kc in range(8):
                k.mm(pr[:, :NT], wr_bf[:, kc * 512 + c4 * 128: kc * 512 + (c4 + 1) * 128],
                     xbs[kc], start=(kc == 0), stop=(kc == 7))
            k.copy(hbf[:, c4 * NT:(c4 + 1) * NT], pr[:, :NT], eng="vector")
            k.dma("gpsimd", res_q[c4 * 128:(c4 + 1) * 128, tok],
                  hbf[:, c4 * NT:(c4 + 1) * NT])
    for t in range(S // NT):
        tok = slice(t * NT, (t + 1) * NT)
        for c4 in range(4):
            k.dma("sync", conv_in[:, :], xp_dram[c4 * 128:(c4 + 1) * 128,
                                                 t * NT: t * NT + NT + 3])
            k.tsc(acc[:, :], conv_in[:, 0:NT], cw[:, c4 * 4 + 0: c4 * 4 + 1], ALU.mult)
            for tap in range(1, 4):
                k.stt(acc[:, :], conv_in[:, tap: tap + NT],
                      cw[:, c4 * 4 + tap: c4 * 4 + tap + 1], acc[:, :],
                      ALU.mult, ALU.add)
            k.act(hbf[:, c4 * NT:(c4 + 1) * NT], acc[:, :], AF.Silu,
                  bias=cb[:, c4:c4 + 1])
            k.dma("gpsimd", h_q[c4 * 128:(c4 + 1) * 128, tok],
                  hbf[:, c4 * NT:(c4 + 1) * NT])
        for c4 in range(4):
            k.mm(ps_xd[:, :], xpw_bf[:, c4 * 96:(c4 + 1) * 96],
                 hbf[:, c4 * NT:(c4 + 1) * NT], start=(c4 == 0), stop=(c4 == 3))
        k.copy(xf1[0:96, :], ps_xd[:, :], eng="scalar")
        k.dma("scalar", xdbl[:, tok], xf1[0:96, :])
    return k.finalize()


def build_M2():
    k = KB("M2")
    h_q = k.din("h_q", (512, S), BF16)
    res_q = k.din("res_q", (512, S), BF16)
    xds = [k.din(f"xd{i}", (96, S), F32) for i in range(4)]
    dtw = k.din("dtw", (64, 512), F32)
    dtb = k.din("dtb", (128, 4), F32)
    acol = k.din("acol", (128, 64), F32)
    dcol = k.din("dcol", (128, 4), F32)
    opw = k.din("opw", (512, D), F32)
    mpart = k.dout("mpart", (D, S), F32)

    bc_dram = k.dtmp("M2_bc", (32, S), F32)

    TC = 2048
    dtw_bf = k.sb((64, 512), BF16, "M2_dtw")
    dtb_c = k.sb((128, 4), F32, "M2_dtb")
    a_c = k.sb((128, 64), F32, "M2_a")
    d_c = k.sb((128, 4), F32, "M2_d")
    opw_bf = k.sb((128, 4 * D), BF16, "M2_opw")
    wstage = k.sb((128, 512), F32, "M2_wst")
    xd_sb = k.sb((96, TC), F32, "M2_xd")
    xd_t = k.sb((96, TC), F32, "M2_xdt")
    dlt_bf = k.sb((64, TC), BF16, "M2_dlt")
    brow = k.sb((1, TC), F32, "M2_brow")
    bb = k.sb((128, TC), F32, "M2_bb")
    cb_ = k.sb((128, TC), F32, "M2_cb")
    delta = k.sb((128, TC), F32, "M2_del")
    wdh = k.sb((128, TC), F32, "M2_w")
    da = k.sb((128, TC), F32, "M2_da")
    dbu = k.sb((128, TC), F32, "M2_dbu")
    st = k.sb((128, TC), F32, "M2_st")
    yac = k.sb((128, TC), F32, "M2_y")
    hch = k.sb((128, TC), BF16, "M2_h")
    rch = k.sb((128, TC), BF16, "M2_r")
    y2 = k.sb((128, 4 * TC), BF16, "M2_y2")
    carry = k.sb((128, 64), F32, "M2_cy")
    sfp = k.sb((128, TC), F32, "M2_sf")
    ones_row_f = k.sb((1, 128), F32, "M2_or")
    pool = [k.psum((128, 512), F32, f"M2_pp{i}") for i in range(4)]

    k.dma("sync", dtb_c[:, :], dtb[:, :])
    k.dma("sync", a_c[:, :], acol[:, :])
    k.act(a_c[:, :], a_c[:, :], AF.Exp)
    k.tsc(a_c[:, :], a_c[:, :], -1.0, ALU.mult)
    k.dma("sync", d_c[:, :], dcol[:, :])
    k.dma("sync", wstage[0:64, 0:512], dtw[:, :])
    k.copy(dtw_bf[:, :], wstage[0:64, 0:512], eng="vector")
    for c4 in range(4):
        k.dma("sync", wstage[:, :512], opw[c4 * 128:(c4 + 1) * 128, 0:512])
        k.copy(opw_bf[:, c4 * D: c4 * D + 512], wstage[:, :512], eng="vector")
        k.dma("sync", wstage[:, :512], opw[c4 * 128:(c4 + 1) * 128, 512:1024])
        k.copy(opw_bf[:, c4 * D + 512: (c4 + 1) * D], wstage[:, :512], eng="vector")
    k.memset(ones_row_f[:, :], 1.0)
    k.memset(carry[:, :], 0.0)

    for tc in range(S // TC):
        tok = slice(tc * TC, (tc + 1) * TC)
        k.dma("sync", xd_sb[:, :], xds[0][:, tok])
        for i in range(1, 4):
            k.dma("gpsimd", xd_t[:, :], xds[i][:, tok])
            k.tt(xd_sb[:, :], xd_sb[:, :], xd_t[:, :], ALU.add)
        k.dma("scalar", bc_dram[:, tok], xd_sb[64:96, :])
        k.copy(dlt_bf[:, :], xd_sb[0:64, :], eng="vector")
        for c4 in range(4):
            for sub in range(TC // 512):
                p = pool[sub % 2]
                k.mm(p[:, :512], dtw_bf[:, c4 * 128:(c4 + 1) * 128],
                     dlt_bf[:, sub * 512:(sub + 1) * 512], start=True, stop=True)
                k.tsc(delta[:, sub * 512:(sub + 1) * 512], p[:, :512],
                      dtb_c[:, c4:c4 + 1], ALU.add)
            z = delta[:, :]
            k.act(sfp[:, :], z, AF.Abs)
            k.act(sfp[:, :], sfp[:, :], AF.Exp, scale=-1.0)
            k.act(sfp[:, :], sfp[:, :], AF.Ln, bias=1.0)
            k.tsc(delta[:, :], z, 0.0, ALU.max)
            k.tt(delta[:, :], delta[:, :], sfp[:, :], ALU.add)
            k.dma("sync", hch[:, :], h_q[c4 * 128:(c4 + 1) * 128, tok])
            k.tt(wdh[:, :], delta[:, :], hch[:, :], ALU.mult)
            for n in range(16):
                for (row, dst) in ((n, bb), (16 + n, cb_)):
                    k.dma("sync", brow[:, :], bc_dram[row: row + 1, tok])
                    for sub in range(TC // 512):
                        k.mm(pool[2][:, :512], ones_row_f[:, :],
                             brow[:, sub * 512:(sub + 1) * 512], start=True, stop=True)
                        k.copy(dst[:, sub * 512:(sub + 1) * 512], pool[2][:, :512],
                               eng="scalar")
                k.act(da[:, :], delta[:, :], AF.Exp, scale=a_c[:, c4 * 16 + n: c4 * 16 + n + 1])
                k.tt(dbu[:, :], wdh[:, :], bb[:, :], ALU.mult)
                k.op("vector", "tensor_tensor_scan", reads=(da, dbu, carry),
                     writes=(st,), out=st[:, :], data0=da[:, :], data1=dbu[:, :],
                     initial=carry[:, c4 * 16 + n: c4 * 16 + n + 1],
                     op0=ALU.mult, op1=ALU.add)
                k.copy(carry[:, c4 * 16 + n: c4 * 16 + n + 1], st[:, TC - 1: TC],
                       eng="vector")
                k.tt(da[:, :], st[:, :], cb_[:, :], ALU.mult, eng="gpsimd")
                if n == 0:
                    k.copy(yac[:, :], da[:, :], eng="gpsimd")
                else:
                    k.tt(yac[:, :], yac[:, :], da[:, :], ALU.add, eng="gpsimd")
            k.dma("sync", rch[:, :], res_q[c4 * 128:(c4 + 1) * 128, tok])
            k.stt(wdh[:, :], hch[:, :], d_c[:, c4:c4 + 1], yac[:, :], ALU.mult, ALU.add)
            k.act(sfp[:, :], rch[:, :], AF.Silu)
            k.tt(y2[:, c4 * TC:(c4 + 1) * TC], wdh[:, :], sfp[:, :], ALU.mult)
        for oc in range(8):
            for sub in range(TC // 512):
                p = pool[oc % 2]
                for c4 in range(4):
                    k.mm(p[:, :512],
                         opw_bf[:, c4 * D + oc * 128: c4 * D + (oc + 1) * 128],
                         y2[:, c4 * TC + sub * 512: c4 * TC + (sub + 1) * 512],
                         start=(c4 == 0), stop=(c4 == 3))
                k.copy(wstage[:, :512], p[:, :512], eng="scalar")
                k.dma("scalar",
                      mpart[oc * 128:(oc + 1) * 128,
                            tc * TC + sub * 512: tc * TC + (sub + 1) * 512],
                      wstage[:, :512])
    return k.finalize()


def run_M(x4_fm, W):
    maps = []
    ip = np.asarray(W["L1_attn_in_proj_w"], np.float32)
    cwf = np.asarray(W["L1_attn_conv_w"], np.float32)[:, 0, :]
    cbf = np.asarray(W["L1_attn_conv_b"], np.float32)
    xpw = np.asarray(W["L1_attn_x_proj_w"], np.float32)
    for c in range(NCORES):
        b, qd = c // 4, c % 4
        sl = slice(qd * 512, (qd + 1) * 512)
        m = {
            "x4full": x4_fm[b],
            "ipw_x": np.ascontiguousarray(ip[:, sl]),
            "ipw_r": np.ascontiguousarray(ip[:, 2048 + qd * 512: 2048 + (qd + 1) * 512]),
            "convw": np.ascontiguousarray(
                cwf[sl].reshape(4, 128, 4).transpose(1, 0, 2).reshape(128, 16)),
            "convb": np.ascontiguousarray(cbf[sl].reshape(4, 128).T),
            "xpw": np.ascontiguousarray(xpw[sl, :]),
        }
        maps.append(m)
    res = _run("M1", build_M1, maps)
    maps2 = []
    dtw = np.asarray(W["L1_attn_dt_proj_w"], np.float32)
    dtb = np.asarray(W["L1_attn_dt_proj_b"], np.float32)
    alog = np.asarray(W["L1_attn_A_log"], np.float32)
    dp = np.asarray(W["L1_attn_D_param"], np.float32)
    opw = np.asarray(W["L1_attn_out_proj_w"], np.float32)
    for c in range(NCORES):
        b, qd = c // 4, c % 4
        sl = slice(qd * 512, (qd + 1) * 512)
        aneg = -np.exp(alog[sl].astype(np.float64)).astype(np.float64)
        m = {
            "h_q": res[c]["h_q"],
            "res_q": res[c]["res_q"],
            "dtw": np.ascontiguousarray(dtw[:, sl]),
            "dtb": np.ascontiguousarray(dtb[sl].reshape(4, 128).T),
            "acol": np.ascontiguousarray(
                alog[sl].reshape(4, 128, 16).transpose(1, 0, 2).reshape(128, 64)),
            "dcol": np.ascontiguousarray(dp[sl].reshape(4, 128).T),
            "opw": np.ascontiguousarray(opw[sl, :]),
        }
        for i in range(4):
            m[f"xd{i}"] = res[b * 4 + i]["xdbl"]
        maps2.append(m)
    res2 = _run("M2", build_M2, maps2)
    return [[res2[b * 4 + i]["mpart"] for i in range(4)] for b in range(2)]


def kernel(**inputs):
    W = inputs
    x0 = np.asarray(inputs["L0_attn_x"], np.float32)
    x0_fm = [np.ascontiguousarray(x0[b].T) for b in range(2)]
    parts1 = run_F(x0_fm, W)
    x2_fm, _x1 = run_A(x0_fm, parts1, W)
    parts2 = run_F(x2_fm, W)
    x4_fm, _r3 = run_LT_chain(x2_fm, parts2, W)
    mparts = run_M(x4_fm, W)
    x6_fm, _r5 = run_LT_chain(x4_fm, mparts, W)
    out = np.stack([x6_fm[b].T for b in range(2)], axis=0)
    return np.ascontiguousarray(out, dtype=np.float32)


# version 05
# speedup vs baseline: 1.2109x; 1.2109x over previous
import math
import numpy as np
from contextlib import ExitStack

import concourse.bass as bass
import concourse.mybir as mybir
from concourse.bass_utils import run_bass_kernel_spmd

F32 = mybir.dt.float32
BF16 = mybir.dt.bfloat16
AF = mybir.ActivationFunctionType
ALU = mybir.AluOpType
AX = mybir.AxisListType

NCORES = 8
D = 1024
S = 8192
TOK = 2048
NCH = D // 128

_COMPILED = {}


class KB:
    ENGS = ("sync", "tensor", "vector", "scalar", "gpsimd")
    DMA_QUEUES = ("sync", "scalar", "gpsimd")
    NLANES = 6

    def __init__(self, name="k"):
        self.nc = bass.Bass("TRN2", target_bir_lowering=False)
        self.es = ExitStack()
        self.streams = {e: [] for e in self.ENGS}
        self._sbid = 0
        self.esem = {e: self._mk_sem(f"m_{e}") for e in self.ENGS}
        self.ecount = {e: 0 for e in self.ENGS}
        self.lanes = {q: [self._mk_sem(f"L_{q}{i}") for i in range(self.NLANES)]
                      for q in self.DMA_QUEUES}
        self.lane_uses = {q: [0] * self.NLANES for q in self.DMA_QUEUES}
        self.lane_rr = {q: 0 for q in self.DMA_QUEUES}
        self.semh = {}
        for e in self.ENGS:
            self.semh[f"m_{e}"] = self.esem[e]
        for q in self.DMA_QUEUES:
            for i in range(self.NLANES):
                self.semh[f"L_{q}{i}"] = self.lanes[q][i]
        self.last_w = {}
        self.readers = {}
        self.waited = {e: {} for e in self.ENGS}

    def _mk_sem(self, name):
        return self.es.enter_context(self.nc.semaphore(name))

    def din(self, name, shape, dtype=F32):
        return self.nc.dram_tensor(name, list(shape), dtype, kind="ExternalInput").ap()

    def dout(self, name, shape, dtype=F32):
        return self.nc.dram_tensor(name, list(shape), dtype, kind="ExternalOutput").ap()

    def dtmp(self, name, shape, dtype=F32):
        return self.nc.dram_tensor(name, list(shape), dtype, kind="Internal").ap()

    def sb(self, shape, dtype=F32, name=None):
        self._sbid += 1
        return self.es.enter_context(
            self.nc.sbuf_tensor(name or f"sb{self._sbid}", list(shape), dtype))

    def psum(self, shape, dtype=F32, name=None):
        self._sbid += 1
        return self.es.enter_context(
            self.nc.psum_tensor(name or f"ps{self._sbid}", list(shape), dtype))

    @staticmethod
    def _key(ap_or_handle):
        t = getattr(ap_or_handle, "tensor", ap_or_handle)
        return getattr(t, "name", None) or id(t)

    @staticmethod
    def _box(ap):
        t = getattr(ap, "tensor", None)
        if t is None:
            return None
        if "PSum" in type(t).__name__:
            return None
        try:
            shape = t.shape
            if len(shape) < 2:
                return None
            fext = 1
            for s_ in shape[1:]:
                fext *= int(s_)
            off = ap.offset
            if not isinstance(off, int):
                return None
            apl = list(ap.ap)
            if len(apl) != 2:
                return None
            (pst, pcnt), (fst, fcnt) = apl
            if int(pst) != fext or int(fst) != 1:
                return None
            p0 = off // fext
            f0 = off % fext
            return (p0, p0 + int(pcnt), f0, f0 + int(fcnt))
        except Exception:
            return None

    @staticmethod
    def _overlap(a, b):
        if a is None or b is None:
            return True
        return not (a[1] <= b[0] or b[1] <= a[0] or a[3] <= b[2] or b[3] <= a[2])

    MAXENT = 24

    def _entries(self, d, kk):
        return d.setdefault(kk, [])

    def _compact(self, lst):
        if len(lst) <= self.MAXENT:
            return lst
        best = {}
        for box, sname, val in lst:
            if best.get(sname, -1) < val:
                best[sname] = val
        return [(None, s, v) for s, v in best.items()]

    def _need(self, eng, reads, writes, pe_acc=False):
        need = {}

        def add(sname, val):
            if pe_acc and eng == "tensor" and sname == "m_tensor":
                return
            if need.get(sname, 0) < val:
                need[sname] = val

        for ap in reads:
            kk = self._key(ap)
            bx = self._box(ap)
            for box, sname, val in self.last_w.get(kk, ()):
                if self._overlap(bx, box):
                    add(sname, val)
        for ap in writes:
            kk = self._key(ap)
            bx = self._box(ap)
            for box, sname, val in self.last_w.get(kk, ()):
                if self._overlap(bx, box):
                    add(sname, val)
            for box, sname, val in self.readers.get(kk, ()):
                if self._overlap(bx, box):
                    add(sname, val)
        out = []
        for sname, val in need.items():
            w = self.waited[eng].get(sname, -1)
            if val > w:
                out.append((sname, val))
                self.waited[eng][sname] = val
        return out

    def _record(self, reads, writes, prod):
        sname, val = prod
        for ap in reads:
            kk = self._key(ap)
            lst = self._entries(self.readers, kk)
            lst.append((self._box(ap), sname, val))
            self.readers[kk] = self._compact(lst)
        for ap in writes:
            kk = self._key(ap)
            bx = self._box(ap)
            if bx is None:
                self.last_w[kk] = [(None, sname, val)]
                self.readers[kk] = []
            else:
                kept_w = [e for e in self.last_w.get(kk, ())
                          if not self._covers(bx, e[0])]
                kept_w.append((bx, sname, val))
                self.last_w[kk] = self._compact(kept_w)
                self.readers[kk] = [e for e in self.readers.get(kk, ())
                                    if not self._covers(bx, e[0])]

    @staticmethod
    def _covers(bx, other):
        if other is None:
            return False
        return (bx[0] <= other[0] and bx[1] >= other[1] and
                bx[2] <= other[2] and bx[3] >= other[3])

    def op(self, eng, method, reads=(), writes=(), pe_acc=False, **kwargs):
        waits = self._need(eng, reads, writes, pe_acc=pe_acc)
        self.ecount[eng] += 1
        tgt = self.ecount[eng]
        wait_hs = tuple((self.semh[s], v) for s, v in waits)
        sem_h = self.esem[eng]

        def fn(e, method=method, kwargs=kwargs, wait_hs=wait_hs, sem_h=sem_h):
            for h, v in wait_hs:
                e.wait_ge(h, v)
            getattr(e, method)(**kwargs).then_inc(sem_h, 1)

        self.streams[eng].append(fn)
        self._record(reads, writes, (f"m_{eng}", tgt))
        return tgt

    def dma(self, q, out, in_):
        reads, writes = (in_,), (out,)
        waits = self._need(q, reads, writes)
        li = self.lane_rr[q]
        self.lane_rr[q] = (li + 1) % self.NLANES
        lane_sem = self.lanes[q][li]
        lname = f"L_{q}{li}"
        prev = self.lane_uses[q][li] * 16
        if prev > self.waited[q].get(lname, -1):
            waits = list(waits) + [(lname, prev)]
            self.waited[q][lname] = prev
        self.lane_uses[q][li] += 1
        tgt = self.lane_uses[q][li] * 16
        wait_hs = tuple((self.semh[s], v) for s, v in waits)

        def fn(e, out=out, in_=in_, wait_hs=wait_hs, lane_sem=lane_sem):
            for h, v in wait_hs:
                e.wait_ge(h, v)
            e.dma_start(out=out, in_=in_).then_inc(lane_sem, 16)

        self.streams[q].append(fn)
        self._record(reads, writes, (lname, tgt))
        return tgt

    def full_barrier(self):
        vals = ([(f"m_{e}", self.ecount[e]) for e in self.ENGS if self.ecount[e] > 0] +
                [(f"L_{q}{i}", self.lane_uses[q][i] * 16)
                 for q in self.DMA_QUEUES for i in range(self.NLANES)
                 if self.lane_uses[q][i] > 0])
        for eng in self.ENGS:
            pairs = []
            for sname, v in vals:
                if v > self.waited[eng].get(sname, -1):
                    pairs.append((self.semh[sname], v))
                    self.waited[eng][sname] = v
            if pairs:
                def fn(e, pairs=tuple(pairs)):
                    for h, v in pairs:
                        e.wait_ge(h, v)
                self.streams[eng].append(fn)

    class _Scope:
        def __init__(self, kb):
            self.kb = kb
            self.es = ExitStack()

        def sb(self, shape, dtype=F32, name=None):
            self.kb._sbid += 1
            return self.es.enter_context(self.kb.nc.sbuf_tensor(
                name or f"sb{self.kb._sbid}", list(shape), dtype))

        def __enter__(self):
            return self

        def __exit__(self, *a):
            self.es.close()
            self.kb.full_barrier()
            return False

    def scope(self):
        return KB._Scope(self)

    def raw(self, eng, fn):
        self.streams[eng].append(fn)

    def barrier_final(self, eng="gpsimd"):
        vals = ([(self.esem[e], self.ecount[e]) for e in self.ENGS if self.ecount[e] > 0] +
                [(self.lanes[q][i], self.lane_uses[q][i] * 16)
                 for q in self.DMA_QUEUES for i in range(self.NLANES)
                 if self.lane_uses[q][i] > 0])

        def fn(e, vals=vals):
            for h, v in vals:
                e.wait_ge(h, v)

        self.streams[eng].append(fn)

    def mm(self, out, lhsT, rhs, start, stop):
        self.op("tensor", "matmul", reads=(lhsT, rhs), writes=(out,),
                pe_acc=not start, out=out, lhsT=lhsT, rhs=rhs, start=start, stop=stop)

    def act(self, out, in_, func, bias=None, scale=None, accum_out=None, eng="scalar"):
        kw = dict(out=out, in_=in_, func=func)
        reads = [in_]
        if isinstance(bias, float) and bias != 0.0:
            bias = self.constcol(bias, out.shape[0])
        if bias is not None:
            kw["bias"] = bias
            if not isinstance(bias, float):
                reads.append(bias)
        if scale is not None:
            kw["scale"] = scale
            if not isinstance(scale, float):
                reads.append(scale)
        writes = [out]
        if accum_out is not None:
            kw["accum_out"] = accum_out
            writes.append(accum_out)
        self.op(eng, "activation", reads=tuple(reads), writes=tuple(writes), **kw)

    def constcol(self, value, parts=128):
        key = (float(value), parts)
        if not hasattr(self, "_consts"):
            self._consts = {}
        if key not in self._consts:
            buf = self.sb((parts, 1), F32, f"const_{len(self._consts)}")
            self.memset(buf[:, :], float(value))
            self._consts[key] = buf[:, :]
        return self._consts[key]

    def copy(self, out, in_, eng="scalar"):
        if eng == "scalar":
            self.op(eng, "copy", reads=(in_,), writes=(out,), out=out, in_=in_)
        else:
            self.op(eng, "tensor_copy", reads=(in_,), writes=(out,), out=out, in_=in_)

    def tt(self, out, a, b, alu, eng="vector"):
        self.op(eng, "tensor_tensor", reads=(a, b), writes=(out,),
                out=out, in0=a, in1=b, op=alu)

    def stt(self, out, in0, scalar, in1, op0, op1, accum_out=None, eng="vector"):
        reads = [in0, in1]
        if not isinstance(scalar, float):
            reads.append(scalar)
        writes = [out]
        kw = dict(out=out, in0=in0, scalar=scalar, in1=in1, op0=op0, op1=op1)
        if accum_out is not None:
            kw["accum_out"] = accum_out
            writes.append(accum_out)
        self.op(eng, "scalar_tensor_tensor", reads=tuple(reads), writes=tuple(writes), **kw)

    def tsc(self, out, in0, s1, op0, s2=None, op1=None, accum_out=None, eng="vector"):
        reads = [in0]
        for s in (s1, s2):
            if s is not None and not isinstance(s, float):
                reads.append(s)
        writes = [out]
        kw = dict(out=out, in0=in0, scalar1=s1, scalar2=s2, op0=op0)
        if op1 is not None:
            kw["op1"] = op1
        if accum_out is not None:
            kw["accum_out"] = accum_out
            writes.append(accum_out)
        self.op(eng, "tensor_scalar", reads=tuple(reads), writes=tuple(writes), **kw)

    def memset(self, ap, val, eng="vector"):
        self.op(eng, "memset", reads=(), writes=(ap,), ap=ap, constant=val)

    def reduce(self, out, in_, alu, eng="vector", negate=False):
        self.op(eng, "tensor_reduce", reads=(in_,), writes=(out,),
                out=out, in_=in_, axis=AX.X, op=alu, negate=negate)

    def finalize(self):
        self.barrier_final()
        block = self.es.enter_context(self.nc.Block())
        for e in self.ENGS:
            fns = self.streams[e]
            if fns:
                getattr(block, e)(lambda eng, fns=fns: [f(eng) for f in fns])
        self.es.close()
        return self.nc


def cdiv(a, b):
    return (a + b - 1) // b


def build_F():
    k = KB("F")
    xf = k.din("xf", (D, S), F32)
    gw1T = k.din("gw1T", (D, D), F32)
    gb1 = k.din("gb1", (128, 8), F32)
    gw2T = k.din("gw2T", (D, 8), F32)
    gb2 = k.din("gb2", (8, 1), F32)
    w1Tq = k.din("w1Tq", (D, D), F32)
    tw1Tq = k.din("tw1Tq", (8, D, D), F32)
    b1q = k.din("b1q", (128, 8), F32)
    tb1q = k.din("tb1q", (128, 64), F32)
    w2Tq = k.din("w2Tq", (D, D), F32)
    tw2Tq = k.din("tw2Tq", (8, D, D), F32)
    b2h = k.din("b2h", (128, 8), F32)
    tb2 = k.din("tb2", (128, 64), F32)
    addb2 = k.din("addb2", (1, 1), F32)
    part = k.dout("part", (D, S), F32)

    xln_dram = k.dtmp("xln_dram", (D, S), BF16)
    g_dram = k.dtmp("g_dram", (8, 1), F32)

    NT = 512
    ntiles = S // NT

    xstage = k.sb((128, 8 * NT), F32, "xstage")
    xbf = k.sb((128, 8 * NT), BF16, "xbf")
    ln = LNHelper(k, maxN=NT)
    gw1_bf = k.sb((128, 8 * D), BF16, "gw1_bf")
    gw2_bf = k.sb((128, 8 * 8), BF16, "gw2_bf")
    wstage = k.sb((128, D), F32, "wstage")
    hbuf = k.sb((128, 8 * NT), BF16, "hbuf")
    ps_a = [k.psum((128, 512), F32, f"psa{i}") for i in range(4)]
    ps_g = k.psum((8, NT), F32, "ps_g")
    gacc = k.sb((8, 1), F32, "gacc")
    gred = k.sb((8, 1), F32, "gred")
    gcol_all = k.sb((128, 8), F32, "gcol_all")
    grow = k.sb((1, 8), F32, "grow")
    ones_row = k.sb((1, 128), F32, "ones_row")

    k.memset(gacc[:, :], 0.0)
    for kc in range(8):
        k.dma("sync", wstage[:, :], gw1T[kc * 128:(kc + 1) * 128, :])
        k.copy(gw1_bf[:, kc * D:(kc + 1) * D], wstage[:, :], eng="vector")
        k.dma("sync", wstage[:, 0:8], gw2T[kc * 128:(kc + 1) * 128, :])
        k.copy(gw2_bf[:, kc * 8:(kc + 1) * 8], wstage[:, 0:8], eng="vector")
    gb1c = k.sb((128, 8), F32, "gb1c")
    k.dma("sync", gb1c[:, :], gb1[:, :])

    pe = None
    for t in range(ntiles):
        xs = [xstage[:, c * NT:(c + 1) * NT] for c in range(8)]
        xbs = [xbf[:, c * NT:(c + 1) * NT] for c in range(8)]
        for c in range(8):
            k.dma("sync", xs[c], xf[c * 128:(c + 1) * 128, t * NT:(t + 1) * NT])
            k.copy(xbs[c], xs[c], eng="vector")
        ln.stats(k, xs, xbs, NT)
        for c in range(8):
            ln.apply(k, xbs[c], xs[c], NT)
            k.dma("gpsimd", xln_dram[c * 128:(c + 1) * 128, t * NT:(t + 1) * NT], xbs[c])
        for oc in range(8):
            p = ps_a[oc % 4]
            for kc in range(8):
                k.mm(p[:, :NT], gw1_bf[:, kc * D + oc * 128: kc * D + (oc + 1) * 128],
                     xbs[kc], start=(kc == 0), stop=(kc == 7))
            k.act(hbuf[:, oc * NT:(oc + 1) * NT], p[:, :NT], AF.Relu,
                  bias=gb1c[:, oc:oc + 1])
        for kc in range(8):
            k.mm(ps_g[:, :], gw2_bf[:, kc * 8:(kc + 1) * 8],
                 hbuf[:, kc * NT:(kc + 1) * NT], start=(kc == 0), stop=(kc == 7))
        k.reduce(gred[:, :], ps_g[:, :], ALU.add)
        k.tt(gacc[:, :], gacc[:, :], gred[:, :], ALU.add)

    gb2c = k.sb((8, 1), F32, "gb2c")
    k.dma("sync", gb2c[:, :], gb2[:, :])
    k.tsc(gacc[:, :], gacc[:, :], 1.0 / S, ALU.mult)
    k.tt(gacc[:, :], gacc[:, :], gb2c[:, :], ALU.add)
    k.dma("gpsimd", g_dram[:, :], gacc[:, :])
    k.dma("gpsimd", grow[:, :], g_dram.rearrange("e one -> one e"))
    k.memset(ones_row[:, :], 1.0)
    ps_b128 = ln.ps_b
    k.mm(ps_b128[:, 0:8], ones_row[:, :], grow[:, :], start=True, stop=True)
    k.copy(gcol_all[:, :], ps_b128[:, 0:8], eng="scalar")

    mw1 = k.sb((128, 8 * D), BF16, "mw1")
    mw2 = k.sb((128, 8 * D), BF16, "mw2")
    wacc = k.sb((128, D), F32, "wacc")
    for (dst, wsrc, twsrc) in ((mw1, w1Tq, tw1Tq), (mw2, w2Tq, tw2Tq)):
        for kc in range(8):
            k.dma("sync", wacc[:, :], wsrc[kc * 128:(kc + 1) * 128, :])
            for e in range(8):
                k.dma("sync", wstage[:, :], twsrc[e, kc * 128:(kc + 1) * 128, :])
                k.stt(wacc[:, :], wstage[:, :], gcol_all[:, e:e + 1], wacc[:, :],
                      ALU.mult, ALU.add)
            k.copy(dst[:, kc * D:(kc + 1) * D], wacc[:, :], eng="vector")

    mb1c = k.sb((128, 8), F32, "mb1c")
    tb1stage = k.sb((128, 8 * 8), F32, "tb1stage")
    k.dma("sync", mb1c[:, :], b1q[:, :])
    k.dma("sync", tb1stage[:, :], tb1q[:, :])
    for e in range(8):
        k.stt(mb1c[:, :], tb1stage[:, e * 8:(e + 1) * 8], gcol_all[:, e:e + 1],
              mb1c[:, :], ALU.mult, ALU.add)
    mb2c = k.sb((128, 8), F32, "mb2c")
    tb2stage = k.sb((128, 8 * 8), F32, "tb2stage")
    k.dma("sync", mb2c[:, :], b2h[:, :])
    k.dma("sync", tb2stage[:, :], tb2[:, :])
    for e in range(8):
        k.stt(mb2c[:, :], tb2stage[:, e * 8:(e + 1) * 8], gcol_all[:, e:e + 1],
              mb2c[:, :], ALU.mult, ALU.add)
    addb2c = k.sb((1, 1), F32, "addb2c")
    k.dma("sync", addb2c[:, :], addb2[:, :])
    addb2_all = k.sb((128, 1), F32, "addb2_all")
    k.mm(ps_b128[:, 0:1], ones_row[:, :], addb2c[:, :], start=True, stop=True)
    k.copy(addb2_all[:, :], ps_b128[:, 0:1], eng="scalar")
    halfmb2 = k.sb((128, 8), F32, "halfmb2")
    k.tsc(mb2c[:, :], mb2c[:, :], addb2_all[:, 0:1], ALU.mult)
    k.tsc(halfmb2[:, :], mb2c[:, :], 0.5, ALU.mult)

    ubuf = k.sb((128, 8 * NT), BF16, "ubuf")
    outstage = k.sb((128, 8 * NT), F32, "outstage")
    for t in range(ntiles):
        xbs = [xbf[:, c * NT:(c + 1) * NT] for c in range(8)]
        for c in range(8):
            k.dma("sync", xbs[c], xln_dram[c * 128:(c + 1) * 128, t * NT:(t + 1) * NT])
        for fc in range(8):
            p = ps_a[fc % 4]
            for kc in range(8):
                k.mm(p[:, :NT], mw1[:, kc * D + fc * 128: kc * D + (fc + 1) * 128],
                     xbs[kc], start=(kc == 0), stop=(kc == 7))
            k.act(ubuf[:, fc * NT:(fc + 1) * NT], p[:, :NT], AF.Gelu_apprx_tanh,
                  bias=mb1c[:, fc:fc + 1])
        for hc in range(8):
            p = ps_a[hc % 4]
            for fc in range(8):
                k.mm(p[:, :NT], mw2[:, fc * D + hc * 128: fc * D + (hc + 1) * 128],
                     ubuf[:, fc * NT:(fc + 1) * NT], start=(fc == 0), stop=(fc == 7))
            k.stt(outstage[:, hc * NT:(hc + 1) * NT], p[:, :NT], 0.5,
                  halfmb2[:, hc:hc + 1].to_broadcast([128, NT]), ALU.mult, ALU.add)
        for hc in range(8):
            k.dma("gpsimd", part[hc * 128:(hc + 1) * 128, t * NT:(t + 1) * NT],
                  outstage[:, hc * NT:(hc + 1) * NT])
    return k.finalize()


class LNHelper:
    def __init__(self, k, maxN=512, name="ln", ps_b=None, tmp=None):
        self.maxN = maxN
        self._ext_tmp = tmp
        self.ones_col_bf = k.sb((128, 1), BF16, f"{name}_oc")
        self.ones_row_f = k.sb((1, 128), F32, f"{name}_or")
        self.sq_bf = k.sb((128, maxN), BF16, f"{name}_sq")
        self.ps_s = k.psum((1, maxN), F32, f"{name}_ps_s")
        self.ps_q = k.psum((1, maxN), F32, f"{name}_ps_q")
        self.ps_b = ps_b if ps_b is not None else k.psum((128, maxN), F32, f"{name}_ps_b")
        self.row = k.sb((1, 3 * maxN), F32, f"{name}_row")
        self.mu_b = k.sb((128, maxN), F32, f"{name}_mu")
        self.rstd_b = k.sb((128, maxN), F32, f"{name}_rs")
        self.tmp = tmp if tmp is not None else k.sb((128, maxN), F32, f"{name}_tmp")
        k.memset(self.ones_col_bf[:, :], 1.0)
        k.memset(self.ones_row_f[:, :], 1.0)

    def stats_chunk(self, k, c, nc_total, x_f32, x_bf16, N):
        k.mm(self.ps_s[:, :N], self.ones_col_bf[:, :], x_bf16,
             start=(c == 0), stop=(c == nc_total - 1))
        k.act(self.sq_bf[:, :N], x_f32, AF.Square)
        k.mm(self.ps_q[:, :N], self.ones_col_bf[:, :], self.sq_bf[:, :N],
             start=(c == 0), stop=(c == nc_total - 1))

    def finish_stats(self, k, N, eps=1e-5, nfeat=1024):
        mu = self.row[:, 0:N]
        var = self.row[:, self.maxN:self.maxN + N]
        rstd = var
        t2 = self.row[:, 2 * self.maxN:2 * self.maxN + N]
        inv_n = 1.0 / nfeat
        k.act(mu, self.ps_s[:, :N], AF.Copy, scale=inv_n)
        k.act(t2, mu, AF.Square)
        k.act(var, self.ps_q[:, :N], AF.Copy, scale=inv_n)
        k.tt(var, var, t2, ALU.subtract)
        k.act(rstd, var, AF.Ln, bias=eps)
        k.act(rstd, rstd, AF.Exp, scale=-0.5)
        k.mm(self.ps_b[:, :N], self.ones_row_f[:, :], mu, start=True, stop=True)
        k.copy(self.mu_b[:, :N], self.ps_b[:, :N], eng="scalar")
        k.mm(self.ps_b[:, :N], self.ones_row_f[:, :], rstd, start=True, stop=True)
        k.copy(self.rstd_b[:, :N], self.ps_b[:, :N], eng="scalar")

    def stats(self, k, x_chunks_f32, xbf_chunks, N, eps=1e-5, nfeat=1024):
        nc = len(x_chunks_f32)
        for c in range(nc):
            k.mm(self.ps_s[:, :N], self.ones_col_bf[:, :], xbf_chunks[c],
                 start=(c == 0), stop=(c == nc - 1))
        for c in range(nc):
            k.act(self.sq_bf[:, :N], x_chunks_f32[c], AF.Square)
            k.mm(self.ps_q[:, :N], self.ones_col_bf[:, :], self.sq_bf[:, :N],
                 start=(c == 0), stop=(c == nc - 1))
        mu = self.row[:, 0:N]
        var = self.row[:, self.maxN:self.maxN + N]
        rstd = var
        t2 = self.row[:, 2 * self.maxN:2 * self.maxN + N]
        inv_n = 1.0 / nfeat
        k.act(mu, self.ps_s[:, :N], AF.Copy, scale=inv_n)
        k.act(t2, mu, AF.Square)
        k.act(var, self.ps_q[:, :N], AF.Copy, scale=inv_n)
        k.tt(var, var, t2, ALU.subtract)
        k.act(rstd, var, AF.Ln, bias=eps)
        k.act(rstd, rstd, AF.Exp, scale=-0.5)
        k.mm(self.ps_b[:, :N], self.ones_row_f[:, :], mu, start=True, stop=True)
        k.copy(self.mu_b[:, :N], self.ps_b[:, :N], eng="scalar")
        k.mm(self.ps_b[:, :N], self.ones_row_f[:, :], rstd, start=True, stop=True)
        k.copy(self.rstd_b[:, :N], self.ps_b[:, :N], eng="scalar")

    def apply(self, k, out_bf, x_f32, N, gcol=None, bcol=None):
        k.tt(self.tmp[:, :N], x_f32, self.mu_b[:, :N], ALU.subtract)
        if gcol is None:
            k.tt(out_bf, self.tmp[:, :N], self.rstd_b[:, :N], ALU.mult)
        else:
            k.tt(self.tmp[:, :N], self.tmp[:, :N], self.rstd_b[:, :N], ALU.mult)
            k.stt(out_bf, self.tmp[:, :N], gcol, bcol, ALU.mult, ALU.add)


def _run(name, builder, in_maps):
    if name not in _COMPILED:
        _COMPILED[name] = builder()
    nc = _COMPILED[name]
    res = run_bass_kernel_spmd(nc, in_maps, core_ids=list(range(NCORES)))
    return res.results


def _fm(a):
    a = np.asarray(a, dtype=np.float32)
    if a.ndim == 3:
        a = a.reshape(-1, a.shape[-1])
    return np.ascontiguousarray(a.T)


def stage_F(x_fm, W):
    maps = []
    for c in range(NCORES):
        b, fq = c // 4, c % 4
        sl = slice(fq * 1024, (fq + 1) * 1024)
        m = {
            "xf": x_fm[b],
            "gw1T": np.ascontiguousarray(W["L0_ffn_gate_w1"].T),
            "gb1": np.asarray(W["L0_ffn_gate_b1"]).reshape(8, 128).T,
            "gw2T": np.ascontiguousarray(W["L0_ffn_gate_w2"].T),
            "gb2": W["L0_ffn_gate_b2"].reshape(-1, 1),
            "w1Tq": np.ascontiguousarray(W["L0_ffn_W1"][sl, :].T),
            "tw1Tq": np.ascontiguousarray(np.transpose(W["L0_ffn_TW1"][:, sl, :], (0, 2, 1))),
            "b1q": np.asarray(W["L0_ffn_b1"])[sl].reshape(8, 128).T,
            "tb1q": np.transpose(np.asarray(W["L0_ffn_Tb1"])[:, sl].reshape(8, 8, 128), (2, 0, 1)).reshape(128, 64),
            "w2Tq": np.ascontiguousarray(W["L0_ffn_W2"][:, sl].T),
            "tw2Tq": np.ascontiguousarray(np.transpose(W["L0_ffn_TW2"][:, :, sl], (0, 2, 1))),
            "b2h": np.asarray(W["L0_ffn_b2"]).reshape(8, 128).T,
            "tb2": np.transpose(np.asarray(W["L0_ffn_Tb2"]).reshape(8, 8, 128), (2, 0, 1)).reshape(128, 64),
            "addb2": np.array([[1.0 if fq == 0 else 0.0]], dtype=np.float32),
        }
        maps.append({kk: np.ascontiguousarray(vv, dtype=np.float32) for kk, vv in m.items()})
    return maps


def run_F(x_fm, W):
    maps = stage_F(x_fm, W)
    res = _run("F", build_F, maps)
    return [[res[b * 4 + q]["part"] for q in range(4)] for b in range(2)]





def build_AK():
    k = KB("AK")
    xbase = k.din("xbase", (D, TOK), F32)
    parts = [k.din(f"p{i}", (D, TOK), F32) for i in range(4)]
    wkT = k.din("wkT", (D, D), F32)
    wvT = k.din("wvT", (D, D), F32)
    bk = k.din("bk", (128, 8), F32)
    bv = k.din("bv", (1, D), F32)
    x1own = k.dout("x1own", (D, TOK), F32)
    xlnown = k.dout("xlnown", (D, TOK), BF16)
    kown = k.dout("kown", (D, TOK), BF16)
    vown = k.dout("vown", (TOK, D), BF16)

    NT = 512
    ln = LNHelper(k, maxN=NT, name="AKln")
    xstage = k.sb((128, 8 * NT), F32, "AK_xst")
    xtile2 = k.sb((128, 8 * NT), F32, "AK_xt2")
    xbf = k.sb((128, 8 * NT), BF16, "AK_xbf")
    outbf = k.sb((128, 8 * NT), BF16, "AK_obf")
    wk_bf = k.sb((128, 8 * D), BF16, "AK_wk")
    wv_bf = k.sb((128, 8 * D), BF16, "AK_wv")
    wstage = k.sb((128, D), F32, "AK_wst")
    bkc = k.sb((128, 8), F32, "AK_bk")
    bvrow = k.sb((1, D), F32, "AK_bvr")
    bvb = k.sb((128, D), F32, "AK_bvb")
    pool = [k.psum((128, 512), F32, f"AK_pp{i}") for i in range(4)]

    k.dma("sync", bkc[:, :], bk[:, :])
    k.dma("sync", bvrow[:, :], bv[:, :])
    for half in range(2):
        k.mm(ln.ps_b[:, :NT], ln.ones_row_f[:, :], bvrow[:, half * 512:(half + 1) * 512],
             start=True, stop=True)
        k.copy(bvb[:, half * 512:(half + 1) * 512], ln.ps_b[:, :NT], eng="scalar")
    for (wt, dst) in ((wkT, wk_bf), (wvT, wv_bf)):
        for kc in range(8):
            k.dma("sync", wstage[:, :], wt[kc * 128:(kc + 1) * 128, :])
            k.copy(dst[:, kc * D:(kc + 1) * D], wstage[:, :], eng="vector")

    for t in range(TOK // NT):
        tok = slice(t * NT, (t + 1) * NT)
        xs = [xstage[:, c * NT:(c + 1) * NT] for c in range(8)]
        xbs = [xbf[:, c * NT:(c + 1) * NT] for c in range(8)]
        for c in range(8):
            rows = slice(c * 128, (c + 1) * 128)
            k.dma("sync", xs[c], xbase[rows, tok])
            for i in range(4):
                k.dma("gpsimd", xtile2[:, c * NT:(c + 1) * NT], parts[i][rows, tok])
                k.tt(xs[c], xs[c], xtile2[:, c * NT:(c + 1) * NT], ALU.add)
            k.dma("scalar", x1own[rows, tok], xs[c])
            k.copy(xbs[c], xs[c], eng="vector")
        ln.stats(k, xs, xbs, NT)
        for c in range(8):
            ln.apply(k, xbs[c], xs[c], NT)
            k.dma("gpsimd", xlnown[c * 128:(c + 1) * 128, tok], xbs[c])
        for oc in range(8):
            p = pool[oc % 2]
            for kc in range(8):
                k.mm(p[:, :NT], wk_bf[:, kc * D + oc * 128:kc * D + (oc + 1) * 128],
                     xbs[kc], start=(kc == 0), stop=(kc == 7))
            k.tsc(xtile2[:, oc * NT:(oc + 1) * NT], p[:, :NT], bkc[:, oc:oc + 1], ALU.add)
            k.copy(outbf[:, oc * NT:(oc + 1) * NT], xtile2[:, oc * NT:(oc + 1) * NT],
                   eng="vector")
            k.dma("gpsimd", kown[oc * 128:(oc + 1) * 128, tok],
                  outbf[:, oc * NT:(oc + 1) * NT])
        for sub in range(4):
            tsub = slice(t * NT + sub * 128, t * NT + (sub + 1) * 128)
            for oh in range(2):
                p = pool[2 + oh]
                for kc in range(8):
                    k.mm(p[:, :NT],
                         xbf[:, kc * NT + sub * 128:kc * NT + (sub + 1) * 128],
                         wv_bf[:, kc * D + oh * 512:kc * D + (oh + 1) * 512],
                         start=(kc == 0), stop=(kc == 7))
                k.tt(xtile2[:, oh * NT:(oh + 1) * NT], p[:, :NT],
                     bvb[:, oh * 512:(oh + 1) * 512], ALU.add)
                k.copy(outbf[:, oh * NT:(oh + 1) * NT], xtile2[:, oh * NT:(oh + 1) * NT],
                       eng="vector")
            k.dma("gpsimd", vown[tsub, :], outbf[:, 0:1024])
    return k.finalize()


def build_AQ():
    k = KB("AQ")
    xlnown = k.din("xlnown", (D, TOK), BF16)
    x1own = k.din("x1own", (D, TOK), F32)
    kfull = k.din("kfull", (D, S), BF16)
    vfull = k.din("vfull", (S, D), BF16)
    wqT = k.din("wqT", (D, D), F32)
    woT = k.din("woT", (D, D), F32)
    bq = k.din("bq", (128, 8), F32)
    bo = k.din("bo", (128, 8), F32)
    x2own = k.dout("x2own", (D, TOK), F32)

    NT = 512
    GQ = 256
    q_sb = k.sb((128, 8 * TOK), BF16, "AQ_q")
    wq_bf = k.sb((128, 8 * D), BF16, "AQ_wq")
    wo_bf = k.sb((128, 8 * D), BF16, "AQ_wo")
    wstage = k.sb((128, D), F32, "AQ_wst")
    xbf = k.sb((128, 8 * NT), BF16, "AQ_xbf")
    bqc = k.sb((128, 8), F32, "AQ_bq")
    boc = k.sb((128, 8), F32, "AQ_bo")
    scores = k.sb((128, S), F32, "AQ_sc")
    cand = k.sb((128, 512), F32, "AQ_cand")
    cand2 = k.sb((128, 512), F32, "AQ_cand2")
    t32 = k.sb((128, 32), F32, "AQ_t32")
    colbuf = k.sb((128, 8), F32, "AQ_cols")
    probs = k.sb((128, S), BF16, "AQ_probs")
    probsT = k.sb((128, 64 * GQ), BF16, "AQ_probsT")
    attn_bf = k.sb((128, 8 * GQ), BF16, "AQ_attn")
    ktile = k.sb((128, 2 * 8 * NT), BF16, "AQ_kt")
    vtile = k.sb((128, 2 * D), BF16, "AQ_vt")
    ps_t = k.psum((128, 512), BF16, "AQ_pt")
    pool = [k.psum((128, 512), F32, f"AQ_pp{i}") for i in range(4)]
    otile = k.sb((128, 2 * GQ), F32, "AQ_ot")
    xres = k.sb((128, GQ), F32, "AQ_xres")

    k.dma("sync", bqc[:, :], bq[:, :])
    k.dma("sync", boc[:, :], bo[:, :])
    for (wt, dst) in ((wqT, wq_bf), (woT, wo_bf)):
        for kc in range(8):
            k.dma("sync", wstage[:, :], wt[kc * 128:(kc + 1) * 128, :])
            k.copy(dst[:, kc * D:(kc + 1) * D], wstage[:, :], eng="vector")
    for t in range(TOK // NT):
        tok = slice(t * NT, (t + 1) * NT)
        for c in range(8):
            k.dma("sync", xbf[:, c * NT:(c + 1) * NT],
                  xlnown[c * 128:(c + 1) * 128, tok])
        for oc in range(8):
            p = pool[oc % 4]
            for kc in range(8):
                k.mm(p[:, :NT], wq_bf[:, kc * D + oc * 128:kc * D + (oc + 1) * 128],
                     xbf[:, kc * NT:(kc + 1) * NT], start=(kc == 0), stop=(kc == 7))
            k.tsc(q_sb[:, oc * TOK + t * NT: oc * TOK + (t + 1) * NT], p[:, :NT],
                  bqc[:, oc:oc + 1], ALU.add)
    for gq in range(TOK // GQ):
        for qt2 in range(GQ // 128):
            qt = gq * 2 + qt2
            qrows = slice(qt * 128, (qt + 1) * 128)
            for kt in range(S // NT):
                ksl = slice(kt * NT, (kt + 1) * NT)
                kb = ktile[:, (kt % 2) * 8 * NT:(kt % 2 + 1) * 8 * NT]
                for c in range(8):
                    k.dma("sync", kb[:, c * NT:(c + 1) * NT],
                          kfull[c * 128:(c + 1) * 128, ksl])
                p = pool[kt % 2]
                for kc in range(8):
                    k.mm(p[:, :NT],
                         q_sb[:, kc * TOK + qt * 128: kc * TOK + (qt + 1) * 128],
                         kb[:, kc * NT:(kc + 1) * NT],
                         start=(kc == 0), stop=(kc == 7))
                k.act(scores[:, ksl], p[:, :NT], AF.Copy, scale=1.0 / 32.0)
            for kc64 in range(64):
                k.op("vector", "max", reads=(scores[:, kc64 * 128:(kc64 + 1) * 128],),
                     writes=(cand[:, kc64 * 8:(kc64 + 1) * 8],),
                     out=cand[:, kc64 * 8:(kc64 + 1) * 8],
                     in_=scores[:, kc64 * 128:(kc64 + 1) * 128])
            cur = cand
            other = cand2
            for r in range(4):
                k.op("vector", "max", reads=(cur[:, :],),
                     writes=(t32[:, r * 8:(r + 1) * 8],),
                     out=t32[:, r * 8:(r + 1) * 8], in_=cur[:, :])
                if r < 3:
                    k.op("vector", "match_replace",
                         reads=(t32[:, r * 8:(r + 1) * 8], cur[:, :]),
                         writes=(other[:, :],),
                         out=other[:, :], in_to_replace=t32[:, r * 8:(r + 1) * 8],
                         in_values=cur[:, :], imm_value=-1e30)
                    cur, other = other, cur
            negm = colbuf[:, 0:1]
            texp = colbuf[:, 1:2]
            den = colbuf[:, 2:3]
            recip = colbuf[:, 3:4]
            k.tsc(negm, t32[:, 0:1], -1.0, ALU.mult)
            k.tt(texp, t32[:, 31:32], t32[:, 0:1], ALU.subtract)
            k.act(texp, texp, AF.Exp)
            k.act(scores[:, :], scores[:, :], AF.Exp, bias=negm)
            k.stt(probs[:, :], scores[:, :], texp, scores[:, :], ALU.is_ge, ALU.mult,
                  accum_out=den)
            k.op("vector", "reciprocal", reads=(den,), writes=(recip,),
                 out=recip, in_=den)
            k.tsc(probs[:, :], probs[:, :], recip, ALU.mult)
            for kc64 in range(64):
                slot = kc64 % 2
                k.op("tensor", "transpose", reads=(probs[:, kc64 * 128:(kc64 + 1) * 128],),
                     writes=(ps_t[:, (qt2 * 2 + slot) * 128:(qt2 * 2 + slot + 1) * 128],),
                     out=ps_t[:, (qt2 * 2 + slot) * 128:(qt2 * 2 + slot + 1) * 128],
                     in_=probs[:, kc64 * 128:(kc64 + 1) * 128],
                     identity=ident_bf(k))
                k.copy(probsT[:, kc64 * GQ + qt2 * 128: kc64 * GQ + (qt2 + 1) * 128],
                       ps_t[:, (qt2 * 2 + slot) * 128:(qt2 * 2 + slot + 1) * 128],
                       eng="scalar" if slot else "vector")
        for half in range(2):
            for kc64 in range(64):
                vb = vtile[:, (kc64 % 2) * D:(kc64 % 2 + 1) * D]
                k.dma("sync", vb, vfull[kc64 * 128:(kc64 + 1) * 128, :])
                for ec in range(4):
                    e0 = half * 4 + ec
                    k.mm(pool[ec][:, :GQ], vb[:, e0 * 128:(e0 + 1) * 128],
                         probsT[:, kc64 * GQ:(kc64 + 1) * GQ],
                         start=(kc64 == 0), stop=(kc64 == 63))
            for ec in range(4):
                e0 = half * 4 + ec
                k.copy(attn_bf[:, e0 * GQ:(e0 + 1) * GQ], pool[ec][:, :GQ], eng="vector")
        for oc in range(8):
            p = pool[oc % 4]
            for ec in range(8):
                k.mm(p[:, :GQ], wo_bf[:, ec * D + oc * 128:ec * D + (oc + 1) * 128],
                     attn_bf[:, ec * GQ:(ec + 1) * GQ], start=(ec == 0), stop=(ec == 7))
            k.dma("sync", xres[:, :], x1own[oc * 128:(oc + 1) * 128,
                                            gq * GQ:(gq + 1) * GQ])
            k.tt(otile[:, (oc % 2) * GQ:(oc % 2) * GQ + GQ], p[:, :GQ], xres[:, :],
                 ALU.add)
            k.tsc(otile[:, (oc % 2) * GQ:(oc % 2) * GQ + GQ],
                  otile[:, (oc % 2) * GQ:(oc % 2) * GQ + GQ], boc[:, oc:oc + 1], ALU.add)
            k.dma("gpsimd", x2own[oc * 128:(oc + 1) * 128, gq * GQ:(gq + 1) * GQ],
                  otile[:, (oc % 2) * GQ:(oc % 2) * GQ + GQ])
    return k.finalize()


_IDENT = {}


def ident_bf(k):
    if id(k) not in _IDENT:
        buf = k.sb((128, 128), BF16, "identbf")
        idram = k.din("ident_in", (128, 128), F32)
        st = k.sb((128, 128), F32, "identst")
        k.dma("sync", st[:, :], idram[:, :])
        k.copy(buf[:, :], st[:, :], eng="vector")
        _IDENT[id(k)] = buf[:, :]
    return _IDENT[id(k)]


def stage_AK(xbase_fm, parts, W):
    maps = []
    for c in range(NCORES):
        b, qc = c // 4, c % 4
        tok = slice(qc * TOK, (qc + 1) * TOK)
        m = {
            "xbase": xbase_fm[b][:, tok],
            "wkT": np.ascontiguousarray(np.asarray(W["L0_attn_Wk"]).T),
            "wvT": np.ascontiguousarray(np.asarray(W["L0_attn_Wv"]).T),
            "bk": np.asarray(W["L0_attn_bk"]).reshape(8, 128).T,
            "bv": np.asarray(W["L0_attn_bv"]).reshape(1, D),
        }
        for i in range(4):
            m[f"p{i}"] = parts[b][i][:, tok]
        maps.append({kk: np.ascontiguousarray(vv, dtype=np.float32) for kk, vv in m.items()})
    return maps


def run_A(xbase_fm, parts, W):
    res = _run("AK", build_AK, stage_AK(xbase_fm, parts, W))
    x1 = [np.concatenate([res[b * 4 + q]["x1own"] for q in range(4)], axis=1)
          for b in range(2)]
    xln = [np.concatenate([res[b * 4 + q]["xlnown"] for q in range(4)], axis=1)
           for b in range(2)]
    kf = [np.concatenate([res[b * 4 + q]["kown"] for q in range(4)], axis=1)
          for b in range(2)]
    vf = [np.concatenate([res[b * 4 + q]["vown"] for q in range(4)], axis=0)
          for b in range(2)]
    maps = []
    ident = np.eye(128, dtype=np.float32)
    for c in range(NCORES):
        b, qc = c // 4, c % 4
        tok = slice(qc * TOK, (qc + 1) * TOK)
        m = {
            "xlnown": xln[b][:, tok],
            "x1own": np.ascontiguousarray(x1[b][:, tok]),
            "kfull": kf[b],
            "vfull": vf[b],
            "wqT": np.ascontiguousarray(np.asarray(W["L0_attn_Wq"]).T.astype(np.float32)),
            "woT": np.ascontiguousarray(np.asarray(W["L0_attn_Wo"]).T.astype(np.float32)),
            "bq": np.ascontiguousarray(np.asarray(W["L0_attn_bq"]).reshape(8, 128).T.astype(np.float32)),
            "bo": np.ascontiguousarray(np.asarray(W["L0_attn_bo"]).reshape(8, 128).T.astype(np.float32)),
            "ident_in": ident,
        }
        maps.append(m)
    res2 = _run("AQ", build_AQ, maps)
    x2 = [np.concatenate([res2[b * 4 + q]["x2own"] for q in range(4)], axis=1)
          for b in range(2)]
    return x2, x1


EXT = 2560
EXTP = EXT + 1


def build_LT_entry():
    return _build_LT("entry")


def build_LT_mid():
    return _build_LT("mid")


def build_LT_final():
    return _build_LT("final")


def _build_LT(mode):
    k = KB("LT_" + mode)
    NT = 512
    NTIL = EXT // NT

    if mode == "entry":
        xbase = k.din("xbase", (D, EXT), F32)
        parts = [k.din(f"p{i}", (D, EXT), F32) for i in range(4)]
        residout = k.dout("residout", (D, TOK), F32)
        xsrc = k.dtmp("LT_x0", (D, EXT), F32)
    else:
        xsrc = k.din("xext", (D, EXT), F32)
    if mode == "final":
        resid_own = k.din("resid_own", (D, TOK), F32)
        xout = k.dout("xfin", (D, TOK), F32)
    else:
        xout = k.dout("xout", (D, TOK), F32)
    wqkv = k.din("wqkv", (D, 3 * D), F32)
    wo = k.din("wo", (D, D), F32)
    w1p = k.din("w1p", (D, 5632), F32)
    w2p = k.din("w2p", (2816, D), F32)
    b1a = k.din("b1a", (128, 22), F32)
    b1g = k.din("b1g", (128, 22), F32)
    b2c = k.din("b2c", (128, 8), F32)
    lng = k.din("lng", (128, 8), F32)
    lnb = k.din("lnb", (128, 8), F32)
    qscol = k.din("qscol", (128, 1), F32)
    kscol = k.din("kscol", (128, 1), F32)
    qsrow = k.din("qsrow", (1, 64), F32)
    ksrow = k.din("ksrow", (1, 64), F32)
    tbl_cs = k.din("tbl_cs", (128, EXT), F32)
    tbl_ss = k.din("tbl_ss", (128, EXT), F32)
    tbl_ci = k.din("tbl_ci", (128, EXT), F32)
    tbl_si = k.din("tbl_si", (128, EXT), F32)
    rotPT = k.din("rotPT", (128, 128), F32)
    bdiag = k.din("bdiag", (128, 128), F32)
    identb = k.din("identb", (128, 128), BF16)
    masks = k.din("masks", (9, 512, 256), BF16)

    QST = EXT - 256
    xnew_dram = k.dtmp("LT_xnew", (D, EXT), F32)
    v_dram = k.dtmp("LT_v", (EXT, D), BF16)
    u_dram = k.dtmp("LT_u", (2816, TOK), BF16)
    q_dram = k.dtmp("LT_qd", (D, QST), F32)
    k_dram = k.dtmp("LT_kd", (D, EXT), F32)

    x_bf = k.sb((128, 8 * EXTP), BF16, "LT_x")
    u_bf = k.sb((128, 22 * 512), BF16, "LT_ub")
    w2_bf = k.sb((128, 22 * 1024), BF16, "LT_w2")
    attn_bf = k.sb((128, 8 * EXT), BF16, "LT_ab")
    tb0 = k.sb((128, 512), F32, "LT_tb0")
    tb1 = k.sb((128, 512), F32, "LT_tb1")
    rotP_sb = k.sb((128, 128), F32, "LT_rp")
    bd_sb = k.sb((128, 128), F32, "LT_bd")
    id_sb = k.sb((128, 128), BF16, "LT_id")
    qtile = k.sb((64, 2 * 256), F32, "LT_qt")
    ktile = k.sb((64, 2 * 512), F32, "LT_kt")
    qraw = k.sb((128, 2 * 512), F32, "LT_qraw")
    ta2 = k.sb((128, 2 * 512), F32, "LT_ta2")
    us2 = k.sb((128, 2 * 512), F32, "LT_us2")
    rn2 = k.sb((128, 2 * 512), F32, "LT_rn2")
    wstage = k.sb((128, 512), F32, "LT_wst")
    wcol = k.sb((128, 4096), BF16, "LT_wc")
    xf1 = k.sb((128, 512), F32, "LT_xf1")
    cols = k.sb((128, 64), F32, "LT_cols")
    rowbuf = k.sb((1, 128), F32, "LT_row")
    mqr = k.sb((1, 4), F32, "LT_mq")

    recb = k.sb((64, 2 * 256), F32, "LT_rcb")
    mask_sb = k.sb((128, 4 * 256), BF16, "LT_msk")

    probsT = k.sb((128, 2 * 4 * 256), BF16, "LT_pT")
    sq_bf = k.sb((128, 512), BF16, "LT_sq")
    gelu_t = k.sb((128, 512), BF16, "LT_gel")

    pool = [k.psum((128, 512), F32, f"LT_pp{i}") for i in range(4)]
    ln = LNHelper(k, maxN=NT, name="LTln", ps_b=pool[3], tmp=ta2[:, 0:512])
    recip_sb = ln.row[0:1, 1024:1536]
    ps_bank = k.psum((128, 512), F32, "LT_bank")
    ps_bank2 = k.psum((128, 512), F32, "LT_bank2")
    ps_sc_p = [ps_bank[:, 0:256], ps_bank2[:, 0:256]]
    ps_ao_p = [ps_bank[0:64, 256:512], ps_bank2[0:64, 256:512]]
    ps_den_p = [ln.ps_s[0:1, 256:512], ln.ps_q[0:1, 256:512]]

    qs_c = cols[:, 0:1]
    ks_c = cols[:, 1:2]
    negB = cols[:, 2:3]
    b1a_c = cols[:, 8:30]
    b1g_c = cols[:, 30:52]
    b2_c = cols[:, 52:60]
    lng_c = cols[:, 60:61]
    lngc = k.sb((128, 8), F32, "LT_lng2")
    lnbc = k.sb((128, 8), F32, "LT_lnb2")

    k.dma("sync", qs_c, qscol[:, :])
    k.dma("sync", ks_c, kscol[:, :])
    k.dma("sync", cols[:, 8:30], b1a[:, :])
    k.dma("sync", cols[:, 30:52], b1g[:, :])
    k.dma("sync", cols[:, 52:60], b2c[:, :])
    k.dma("sync", lngc[:, :], lng[:, :])
    k.dma("sync", lnbc[:, :], lnb[:, :])
    k.dma("sync", rotP_sb[:, :], rotPT[:, :])
    k.dma("sync", bd_sb[:, :], bdiag[:, :])
    k.dma("sync", id_sb[:, :], identb[:, :])
    k.dma("sync", rowbuf[:, 0:64], qsrow[:, :])
    k.dma("sync", rowbuf[:, 64:128], ksrow[:, :])

    k.op("vector", "tensor_reduce", reads=(rowbuf[:, 0:64],), writes=(mqr[:, 0:1],),
         out=mqr[:, 0:1], in_=rowbuf[:, 0:64], axis=AX.X, op=ALU.max,
         apply_absolute_value=True)
    k.op("vector", "tensor_reduce", reads=(rowbuf[:, 64:128],), writes=(mqr[:, 1:2],),
         out=mqr[:, 1:2], in_=rowbuf[:, 64:128], axis=AX.X, op=ALU.max,
         apply_absolute_value=True)
    k.tt(mqr[:, 2:3], mqr[:, 0:1], mqr[:, 1:2], ALU.mult)
    k.tsc(mqr[:, 3:4], mqr[:, 2:3], -8.0, ALU.mult)
    k.mm(pool[0][:, 0:1], ln.ones_row_f[:, :], mqr[:, 3:4], start=True, stop=True)
    k.copy(negB, pool[0][:, 0:1], eng="scalar")

    xf2 = qraw[:, 512:1024]

    _lc_idx = [0]

    def loadcast128(src_ap, dst_ap):
        i = _lc_idx[0] % 4
        _lc_idx[0] += 1
        st = wstage[:, i * 128:(i + 1) * 128]
        k.dma("sync", st, src_ap)
        k.copy(dst_ap, st, eng="vector" if i % 2 == 0 else "gpsimd")


    def xsh(c, off, n):
        base = c * EXTP + 1
        if c >= 4:
            return x_bf[:, base + off - 1: base + off - 1 + n]
        return x_bf[:, base + off: base + off + n]

    def xv(c, off, n):
        base = c * EXTP + 1
        return x_bf[:, base + off: base + off + n]

    for c in range(8):
        k.memset(x_bf[:, c * EXTP: c * EXTP + 1], 0.0)
        k.memset(attn_bf[:, c * EXT: c * EXT + 256], 0.0)
    for t in range(NTIL):
        tok = slice(t * NT, (t + 1) * NT)
        if mode == "entry":
            for c in range(8):
                rows = slice(c * 128, (c + 1) * 128)
                k.dma("sync", xf1[:, :], xbase[rows, tok])
                for i in range(4):
                    k.dma("gpsimd", xf2[:, :], parts[i][rows, tok])
                    k.tt(xf1[:, :], xf1[:, :], xf2[:, :], ALU.add)
                k.dma("scalar", xsrc[rows, tok], xf1[:, :])
                if t >= 1:
                    k.dma("scalar", residout[rows, (t - 1) * NT: t * NT], xf1[:, :])
            for c in range(8):
                rows = slice(c * 128, (c + 1) * 128)
                k.dma("sync", xf1[:, :], xsrc[rows, tok])
                k.copy(sq_bf[:, :NT], xf1[:, :], eng="vector")
                k.copy(xv(c, t * NT, NT), sq_bf[:, :NT], eng="vector")
                ln.stats_chunk(k, c, 8, xf1[:, :], xv(c, t * NT, NT), NT)
            ln.finish_stats(k, NT)
            for c in range(8):
                rows = slice(c * 128, (c + 1) * 128)
                k.dma("sync", xf1[:, :], xsrc[rows, tok])
                ln.apply(k, xv(c, t * NT, NT), xf1[:, :], NT)
                k.tt(xf2[:, :], xf1[:, :], ln.mu_b[:, :NT], ALU.subtract)
                k.tt(xf2[:, :], xf2[:, :], ln.rstd_b[:, :NT], ALU.mult)
                k.dma("scalar", xnew_dram[rows, tok], xf2[:, :])
        else:
            for c in range(8):
                rows = slice(c * 128, (c + 1) * 128)
                sl512 = qraw[:, (c % 2) * 512:(c % 2) * 512 + NT]
                k.dma("sync", sl512, xsrc[rows, tok])
                k.copy(xv(c, t * NT, NT), sl512, eng="vector" if c % 2 else "gpsimd")
    res_dram = xnew_dram if mode == "entry" else xsrc

    eps24 = 1e-24
    for qk in range(2):
        scol = qs_c if qk == 0 else ks_c
        tsrc0 = tbl_cs if qk == 0 else tbl_ci
        tsrc1 = tbl_ss if qk == 0 else tbl_si
        for oc in range(8):
            m0 = qk * D + oc * 128
            for kc in range(8):
                loadcast128(wqkv[kc * 128:(kc + 1) * 128, m0:m0 + 128],
                            wcol[:, kc * 128:(kc + 1) * 128])
            for t in range(NTIL):
                par = t % 2
                p = pool[par]
                qr = qraw[:, par * 512: par * 512 + NT]
                tav = ta2[:, par * 512: par * 512 + NT]
                usv = us2[:, par * 512: par * 512 + NT]
                rnv = rn2[:, par * 512: par * 512 + NT]
                psb = pool[2][:, par * 256:(par + 1) * 256] if False else pool[2 + par]
                for kc in range(8):
                    k.mm(p[:, :NT], wcol[:, kc * 128:(kc + 1) * 128],
                         xsh(kc, t * NT, NT), start=(kc == 0), stop=(kc == 7))
                k.copy(qr, p[:, :NT], eng="scalar")
                k.dma("sync", tb0[:, :], tsrc0[:, t * NT:(t + 1) * NT])
                k.dma("sync", tb1[:, :], tsrc1[:, t * NT:(t + 1) * NT])
                k.act(tav, qr, AF.Square)
                k.mm(psb[:, :NT], bd_sb[:, :], tav, start=True, stop=True)
                k.act(rnv, psb[:, :NT], AF.Ln, bias=eps24)
                k.act(rnv, rnv, AF.Exp, scale=-0.5)
                k.mm(psb[:, :NT], rotP_sb[:, :], qr, start=True, stop=True)
                k.stt(usv, qr, scol, tb0[:, :NT], ALU.mult, ALU.mult)
                k.stt(tav, psb[:, :NT], scol, tb1[:, :NT], ALU.mult, ALU.mult)
                k.tt(tav, tav, usv, ALU.add)
                k.tt(qr, tav, rnv, ALU.mult)
                rows = slice(oc * 128, (oc + 1) * 128)
                if qk == 1:
                    k.dma("scalar", k_dram[rows, t * NT:(t + 1) * NT], qr)
                elif t == 0:
                    k.dma("scalar", q_dram[rows, 0:256], qr[:, 256:NT])
                else:
                    k.dma("scalar", q_dram[rows, t * NT - 256:(t + 1) * NT - 256], qr)

    for vc2 in range(2):
        for kc in range(8):
            k.dma("sync", wstage[:, :], wqkv[kc * 128:(kc + 1) * 128,
                                             2 * D + vc2 * 512: 2 * D + (vc2 + 1) * 512])
            k.copy(wcol[:, kc * 512:(kc + 1) * 512], wstage[:, :], eng="vector")
        for t in range(NTIL):
            for sub in range(4):
                p = pool[2 + sub % 2]
                for kc in range(8):
                    k.mm(p[:, :NT], xsh(kc, t * NT + sub * 128, 128),
                         wcol[:, kc * 512:(kc + 1) * 512], start=(kc == 0), stop=(kc == 7))
                k.copy(sq_bf[:, :NT], p[:, :NT], eng="vector")
                k.dma("scalar",
                      v_dram[t * NT + sub * 128: t * NT + (sub + 1) * 128,
                             vc2 * 512:(vc2 + 1) * 512], sq_bf[:, :NT])

    for w in range(1, 10):
        kst = w * 256 - 256
        vtile = wcol
        for kt in range(4):
            k.dma("sync", mask_sb[:, kt * 256:(kt + 1) * 256],
                  masks[w - 1, kt * 128:(kt + 1) * 128, :])
            k.dma("sync", vtile[:, kt * 1024:(kt + 1) * 1024],
                  v_dram[kst + kt * 128: kst + (kt + 1) * 128, :])
        for h in range(16):
            c, hh = h // 2, h % 2
            par = h % 2
            ps_sc = ps_sc_p[par]
            ps_ao = ps_ao_p[par]
            ps_den = ps_den_p[par]
            pT = probsT[:, par * 1024:(par + 1) * 1024]
            qt_ = qtile[:, par * 256:(par + 1) * 256]
            kt_ = ktile[:, par * 512:(par + 1) * 512]
            rc_ = recip_sb[0:1, par * 256:(par + 1) * 256]
            rb_ = recb[:, par * 256:(par + 1) * 256]
            prow = slice(hh * 64, (hh + 1) * 64)
            hrows = slice(c * 128 + hh * 64, c * 128 + (hh + 1) * 64)
            k.dma("sync", qt_, q_dram[hrows, (w - 1) * 256: w * 256])
            for kt in range(4):
                k.dma("sync", kt_[:, kt * 128:(kt + 1) * 128],
                      k_dram[hrows, kst + kt * 128: kst + (kt + 1) * 128])
            for kt in range(4):
                k.mm(ps_sc, kt_[:, kt * 128:(kt + 1) * 128], qt_, start=True, stop=False)
                k.mm(ps_sc, id_sb[:, :], mask_sb[:, kt * 256:(kt + 1) * 256],
                     start=False, stop=True)
                k.act(pT[:, kt * 256:(kt + 1) * 256], ps_sc, AF.Exp,
                      scale=8.0, bias=negB)
            for kt in range(4):
                k.mm(ps_den, ln.ones_col_bf[:, :], pT[:, kt * 256:(kt + 1) * 256],
                     start=(kt == 0), stop=(kt == 3))
            k.op("vector", "reciprocal", reads=(ps_den,), writes=(rc_,),
                 out=rc_, in_=ps_den)
            for kt in range(4):
                k.mm(ps_ao, vtile[:, kt * 1024 + h * 64: kt * 1024 + (h + 1) * 64],
                     pT[:, kt * 256:(kt + 1) * 256], start=(kt == 0), stop=(kt == 3))
            k.mm(ps_sc[0:64, :], ln.ones_row_f[:, 0:64], rc_, start=True, stop=True)
            k.copy(rb_, ps_sc[0:64, :], eng="scalar")
            k.tt(attn_bf[prow, c * EXT + w * 256: c * EXT + (w + 1) * 256],
                 ps_ao, rb_, ALU.mult)

    for oc in range(8):
        rows = slice(oc * 128, (oc + 1) * 128)
        for ec in range(8):
            loadcast128(wo[ec * 128:(ec + 1) * 128, oc * 128:(oc + 1) * 128],
                        wcol[:, ec * 128:(ec + 1) * 128])
        for t in range(NTIL):
            p = pool[t % 2]
            for ec in range(8):
                k.mm(p[:, :NT], wcol[:, ec * 128:(ec + 1) * 128],
                     attn_bf[:, ec * EXT + t * NT: ec * EXT + (t + 1) * NT],
                     start=(ec == 0), stop=(ec == 7))
            k.dma("sync", xf1[:, :], res_dram[rows, t * NT:(t + 1) * NT])
            k.tt(xf2[:, :], p[:, :NT], xf1[:, :], ALU.add)
            k.dma("scalar", xnew_dram[rows, t * NT:(t + 1) * NT], xf2[:, :])
            k.copy(xv(oc, t * NT, NT), xf2[:, :], eng="vector")

    for jc in range(22):
        for oc in range(8):
            loadcast128(w2p[jc * 128:(jc + 1) * 128, oc * 128:(oc + 1) * 128],
                        w2_bf[:, jc * 1024 + oc * 128: jc * 1024 + (oc + 1) * 128])
    for t in range(1, NTIL):
        for c in range(8):
            rows = slice(c * 128, (c + 1) * 128)
            off = t * NT - 1 if c >= 4 else t * NT
            k.dma("sync", xf1[:, :], xnew_dram[rows, off: off + NT])
            ln.stats_chunk(k, c, 8, xf1[:, :], xsh(c, t * NT, NT), NT)
        ln.finish_stats(k, NT)
        for c in range(8):
            rows = slice(c * 128, (c + 1) * 128)
            off = t * NT - 1 if c >= 4 else t * NT
            k.dma("sync", xf1[:, :], xnew_dram[rows, off: off + NT])
            ln.apply(k, attn_bf[:, c * EXT + (t - 1) * NT: c * EXT + t * NT],
                     xf1[:, :], NT, gcol=lngc[:, c:c + 1],
                     bcol=lnbc[:, c:c + 1].to_broadcast([128, NT]))
    for j in range(22):
        for kc in range(8):
            loadcast128(w1p[kc * 128:(kc + 1) * 128, j * 128:(j + 1) * 128],
                        wcol[:, kc * 128:(kc + 1) * 128])
            loadcast128(w1p[kc * 128:(kc + 1) * 128, 2816 + j * 128: 2816 + (j + 1) * 128],
                        wcol[:, 1024 + kc * 128: 1024 + (kc + 1) * 128])
        for t4 in range(4):
            xnsl = [attn_bf[:, c * EXT + t4 * NT: c * EXT + (t4 + 1) * NT] for c in range(8)]
            for kc in range(8):
                k.mm(pool[0][:, :NT], wcol[:, kc * 128:(kc + 1) * 128], xnsl[kc],
                     start=(kc == 0), stop=(kc == 7))
            for kc in range(8):
                k.mm(pool[1][:, :NT], wcol[:, 1024 + kc * 128: 1024 + (kc + 1) * 128],
                     xnsl[kc], start=(kc == 0), stop=(kc == 7))
            k.tsc(ta2[:, :NT], pool[0][:, :NT], b1a_c[:, j:j + 1], ALU.add)
            k.act(gelu_t[:, :NT], pool[1][:, :NT], AF.Gelu, bias=b1g_c[:, j:j + 1])
            k.tt(sq_bf[:, :NT], ta2[:, :NT], gelu_t[:, :NT], ALU.mult)
            k.dma("scalar", u_dram[j * 128:(j + 1) * 128, t4 * NT:(t4 + 1) * NT],
                  sq_bf[:, :NT])
    for t4 in range(4):
        for jc in range(22):
            k.dma("sync", u_bf[:, jc * NT:(jc + 1) * NT],
                  u_dram[jc * 128:(jc + 1) * 128, t4 * NT:(t4 + 1) * NT])
        for oc in range(8):
            rows = slice(oc * 128, (oc + 1) * 128)
            p = pool[oc % 2]
            for jc in range(22):
                k.mm(p[:, :NT], w2_bf[:, jc * 1024 + oc * 128: jc * 1024 + (oc + 1) * 128],
                     u_bf[:, jc * NT:(jc + 1) * NT], start=(jc == 0), stop=(jc == 21))
            k.tsc(ta2[:, 512:512 + NT], p[:, :NT], b2_c[:, oc:oc + 1], ALU.add)
            k.dma("sync", xf1[:, :], xnew_dram[rows, NT + t4 * NT: NT + (t4 + 1) * NT])
            k.tt(xf2[:, :], ta2[:, 512:512 + NT], xf1[:, :], ALU.add)
            if mode == "final":
                k.dma("sync", xf1[:, :], resid_own[rows, t4 * NT:(t4 + 1) * NT])
                k.stt(xf2[:, :], xf2[:, :], 0.5, xf1[:, :], ALU.mult, ALU.add)
            k.dma("scalar", xout[rows, t4 * NT:(t4 + 1) * NT], xf2[:, :])
    return k.finalize()


import ml_dtypes

BF = ml_dtypes.bfloat16


def _ext_slice(full_fm, s):
    lo = s - 512
    out = np.zeros((D, EXT), dtype=np.float32)
    a = max(0, lo)
    out[:, a - lo:] = full_fm[:, a: s + TOK]
    return out


def _lt_tables(s):
    g = np.arange(s - 512, s + TOK, dtype=np.float64)
    half = np.arange(0, 64, 2, dtype=np.float64)
    inv_freq = 1.0 / 10000.0 ** (half / 64.0)
    fr = g[:, None] * inv_freq[None, :]
    fr = np.concatenate([fr, fr], axis=1)
    base_scale = (half + 0.4 * 64) / (1.4 * 64)
    power = (g - 8192 // 2) / (256 // 2)
    scale = base_scale[None, :] ** power[:, None]
    scale = np.concatenate([scale, scale], axis=1)
    cs = np.cos(fr) * scale
    ss = np.sin(fr) * scale
    ci = np.cos(fr) / scale
    si = np.sin(fr) / scale

    def pack(v):
        vt = v.T.astype(np.float32)
        return np.ascontiguousarray(np.vstack([vt, vt]))
    return pack(cs), pack(ss), pack(ci), pack(si)


def _lt_masks(s):
    M = np.zeros((9, 512, 256), dtype=np.float32)
    for w in range(1, 10):
        gws = s - 512 + w * 256
        if gws < 0:
            continue
        it = gws + np.arange(256)[None, :]
        jl = np.arange(512)[:, None]
        jb = np.where(jl < 256,
                      np.where(gws - 256 + jl >= 0, gws - 256 + jl, -1),
                      gws + (jl - 256))
        keep = (jb != -1) & (it >= jb) & (it <= jb + 256)
        M[w - 1] = np.where(keep, 0.0, -30000.0)
    return M.astype(BF)


def _rot_consts():
    P = np.zeros((128, 128), dtype=np.float32)
    for blk in range(2):
        o = blk * 64
        for j in range(32):
            P[o + j, o + j + 32] = -1.0
        for j in range(32, 64):
            P[o + j, o + j - 32] = 1.0
    PT = np.ascontiguousarray(P.T)
    BD = np.zeros((128, 128), dtype=np.float32)
    BD[:64, :64] = 1.0
    BD[64:, 64:] = 1.0
    return PT, BD, np.eye(128, dtype=np.float32).astype(BF)


def _col8(v):
    return np.ascontiguousarray(np.asarray(v, np.float32).reshape(8, 128).T)


def _pad_w1(W1, b1):
    a, g = W1[:, :2730], W1[:, 2730:]
    w1p = np.zeros((D, 5632), np.float32)
    w1p[:, :2730] = a
    w1p[:, 2816:2816 + 2730] = g
    b1a = np.zeros(2816, np.float32)
    b1a[:2730] = b1[:2730]
    b1g = np.zeros(2816, np.float32)
    b1g[:2730] = b1[2730:]
    return (w1p, np.ascontiguousarray(b1a.reshape(22, 128).T),
            np.ascontiguousarray(b1g.reshape(22, 128).T))


def _pad_w2(W2):
    w2p = np.zeros((2816, D), np.float32)
    w2p[:2730, :] = W2
    return w2p


def stage_LT(mode, layer, W, xfull_fm=None, base_fm=None, parts_fm=None,
             resid_own=None):
    i = layer
    wqkv = np.ascontiguousarray(np.asarray(W["L1_ffn_Wqkv"][i], np.float32))
    wo = np.ascontiguousarray(np.asarray(W["L1_ffn_Wo"][i], np.float32))
    w1p, b1a, b1g = _pad_w1(np.asarray(W["L1_ffn_W1"][i], np.float32),
                            np.asarray(W["L1_ffn_b1"][i], np.float32))
    w2p = _pad_w2(np.asarray(W["L1_ffn_W2"][i], np.float32))
    b2c = _col8(W["L1_ffn_b2"][i])
    lng = _col8(W["L1_ffn_ln_g"][i])
    lnb = _col8(W["L1_ffn_ln_b"][i])
    qs = np.asarray(W["L1_ffn_q_scale"][i], np.float32)
    ks = np.asarray(W["L1_ffn_k_scale"][i], np.float32)
    PT, BD, ID = _rot_consts()
    maps = []
    for c in range(NCORES):
        b, qc = c // 4, c % 4
        s = qc * TOK
        cs, ss, ci, si = _lt_tables(s)
        m = {
            "wqkv": wqkv, "wo": wo, "w1p": w1p, "w2p": w2p,
            "b1a": b1a, "b1g": b1g, "b2c": b2c, "lng": lng, "lnb": lnb,
            "qscol": np.tile(qs, 2).reshape(128, 1),
            "kscol": np.tile(ks, 2).reshape(128, 1),
            "qsrow": qs.reshape(1, 64), "ksrow": ks.reshape(1, 64),
            "tbl_cs": cs, "tbl_ss": ss, "tbl_ci": ci, "tbl_si": si,
            "rotPT": PT, "bdiag": BD, "identb": ID,
            "masks": _lt_masks(s),
        }
        if mode == "entry":
            m["xbase"] = _ext_slice(base_fm[b], s)
            for j in range(4):
                m[f"p{j}"] = _ext_slice(parts_fm[b][j], s)
        else:
            m["xext"] = _ext_slice(xfull_fm[b], s)
        if mode == "final":
            m["resid_own"] = np.ascontiguousarray(resid_own[b][:, s:s + TOK])
        maps.append(m)
    return maps


_LT_BUILDERS = {"entry": build_LT_entry, "mid": build_LT_mid, "final": build_LT_final}


def run_LT_chain(base_fm, parts_fm, W):
    res = _run("LT_entry", build_LT_entry,
               stage_LT("entry", 0, W, base_fm=base_fm, parts_fm=parts_fm))
    resid = [np.concatenate([res[b * 4 + q]["residout"] for q in range(4)], axis=1)
             for b in range(2)]
    x = [np.concatenate([res[b * 4 + q]["xout"] for q in range(4)], axis=1)
         for b in range(2)]
    for layer in range(1, 5):
        res = _run("LT_mid", build_LT_mid, stage_LT("mid", layer, W, xfull_fm=x))
        x = [np.concatenate([res[b * 4 + q]["xout"] for q in range(4)], axis=1)
             for b in range(2)]
    res = _run("LT_final", build_LT_final,
               stage_LT("final", 5, W, xfull_fm=x, resid_own=resid))
    xn = [np.concatenate([res[b * 4 + q]["xfin"] for q in range(4)], axis=1)
          for b in range(2)]
    return xn, resid


def build_M1():
    k = KB("M1")
    x4 = k.din("x4full", (D, S), F32)
    ipw_x = k.din("ipw_x", (D, 512), F32)
    ipw_r = k.din("ipw_r", (D, 512), F32)
    convw = k.din("convw", (128, 16), F32)
    convb = k.din("convb", (128, 4), F32)
    xpw = k.din("xpw", (512, 96), F32)
    h_q = k.dout("h_q", (512, S), BF16)
    res_q = k.dout("res_q", (512, S), BF16)
    xdbl = k.dout("xdbl", (96, S), F32)

    xp_dram = k.dtmp("M1_xp", (512, S + 3), F32)

    NT = 512
    ln = LNHelper(k, maxN=NT, name="M1ln")
    xstage = k.sb((128, 8 * NT), F32, "M1_xst")
    xbf = k.sb((128, 8 * NT), BF16, "M1_xbf")
    wx_bf = k.sb((128, 8 * 512), BF16, "M1_wx")
    wr_bf = k.sb((128, 8 * 512), BF16, "M1_wr")
    xpw_bf = k.sb((128, 4 * 96), BF16, "M1_xpw")
    wstage = k.sb((128, 512), F32, "M1_wst")
    cw = k.sb((128, 16), F32, "M1_cw")
    cb = k.sb((128, 4), F32, "M1_cb")
    xf1 = k.sb((128, 512), F32, "M1_f1")
    hbf = k.sb((128, 4 * NT), BF16, "M1_h")
    conv_in = k.sb((128, NT + 3), F32, "M1_ci")
    acc = k.sb((128, NT), F32, "M1_acc")
    pool = [k.psum((128, 512), F32, f"M1_pp{i}") for i in range(4)]
    ps_xd = k.psum((96, 512), F32, "M1_pxd")

    k.dma("sync", cw[:, :], convw[:, :])
    k.dma("sync", cb[:, :], convb[:, :])
    for (wt, dst) in ((ipw_x, wx_bf), (ipw_r, wr_bf)):
        for kc in range(8):
            k.dma("sync", wstage[:, :], wt[kc * 128:(kc + 1) * 128, :])
            k.copy(dst[:, kc * 512:(kc + 1) * 512], wstage[:, :], eng="vector")
    for c4 in range(4):
        k.dma("sync", wstage[:, 0:96], xpw[c4 * 128:(c4 + 1) * 128, :])
        k.copy(xpw_bf[:, c4 * 96:(c4 + 1) * 96], wstage[:, 0:96], eng="vector")
    k.memset(xf1[:, 0:3], 0.0)
    for c4 in range(4):
        k.dma("scalar", xp_dram[c4 * 128:(c4 + 1) * 128, 0:3], xf1[:, 0:3])

    for t in range(S // NT):
        tok = slice(t * NT, (t + 1) * NT)
        xs = [xstage[:, c * NT:(c + 1) * NT] for c in range(8)]
        xbs = [xbf[:, c * NT:(c + 1) * NT] for c in range(8)]
        for c in range(8):
            k.dma("sync", xs[c], x4[c * 128:(c + 1) * 128, tok])
            k.copy(xbs[c], xs[c], eng="vector")
        ln.stats(k, xs, xbs, NT)
        for c in range(8):
            ln.apply(k, xbs[c], xs[c], NT)
        for c4 in range(4):
            p = pool[c4 % 2]
            for kc in range(8):
                k.mm(p[:, :NT], wx_bf[:, kc * 512 + c4 * 128: kc * 512 + (c4 + 1) * 128],
                     xbs[kc], start=(kc == 0), stop=(kc == 7))
            k.copy(xf1[:, :], p[:, :NT], eng="scalar")
            k.dma("scalar", xp_dram[c4 * 128:(c4 + 1) * 128, 3 + t * NT: 3 + (t + 1) * NT],
                  xf1[:, :])
            pr = pool[2 + c4 % 2]
            for kc in range(8):
                k.mm(pr[:, :NT], wr_bf[:, kc * 512 + c4 * 128: kc * 512 + (c4 + 1) * 128],
                     xbs[kc], start=(kc == 0), stop=(kc == 7))
            k.copy(hbf[:, c4 * NT:(c4 + 1) * NT], pr[:, :NT], eng="vector")
            k.dma("gpsimd", res_q[c4 * 128:(c4 + 1) * 128, tok],
                  hbf[:, c4 * NT:(c4 + 1) * NT])
    for t in range(S // NT):
        tok = slice(t * NT, (t + 1) * NT)
        for c4 in range(4):
            k.dma("sync", conv_in[:, :], xp_dram[c4 * 128:(c4 + 1) * 128,
                                                 t * NT: t * NT + NT + 3])
            k.tsc(acc[:, :], conv_in[:, 0:NT], cw[:, c4 * 4 + 0: c4 * 4 + 1], ALU.mult)
            for tap in range(1, 4):
                k.stt(acc[:, :], conv_in[:, tap: tap + NT],
                      cw[:, c4 * 4 + tap: c4 * 4 + tap + 1], acc[:, :],
                      ALU.mult, ALU.add)
            k.act(hbf[:, c4 * NT:(c4 + 1) * NT], acc[:, :], AF.Silu,
                  bias=cb[:, c4:c4 + 1])
            k.dma("gpsimd", h_q[c4 * 128:(c4 + 1) * 128, tok],
                  hbf[:, c4 * NT:(c4 + 1) * NT])
        for c4 in range(4):
            k.mm(ps_xd[:, :], xpw_bf[:, c4 * 96:(c4 + 1) * 96],
                 hbf[:, c4 * NT:(c4 + 1) * NT], start=(c4 == 0), stop=(c4 == 3))
        k.copy(xf1[0:96, :], ps_xd[:, :], eng="scalar")
        k.dma("scalar", xdbl[:, tok], xf1[0:96, :])
    return k.finalize()


def build_M2():
    k = KB("M2")
    h_q = k.din("h_q", (512, S), BF16)
    res_q = k.din("res_q", (512, S), BF16)
    xds = [k.din(f"xd{i}", (96, S), F32) for i in range(4)]
    dtw = k.din("dtw", (64, 512), F32)
    dtb = k.din("dtb", (128, 4), F32)
    acol = k.din("acol", (128, 64), F32)
    dcol = k.din("dcol", (128, 4), F32)
    opw = k.din("opw", (512, D), F32)
    mpart = k.dout("mpart", (D, S), F32)

    bc_dram = k.dtmp("M2_bc", (32, S), F32)

    TC = 2048
    dtw_bf = k.sb((64, 512), BF16, "M2_dtw")
    dtb_c = k.sb((128, 4), F32, "M2_dtb")
    a_c = k.sb((128, 64), F32, "M2_a")
    d_c = k.sb((128, 4), F32, "M2_d")
    opw_bf = k.sb((128, 4 * D), BF16, "M2_opw")
    wstage = k.sb((128, 512), F32, "M2_wst")
    xd_sb = k.sb((96, TC), F32, "M2_xd")
    dlt_bf = k.sb((64, TC), BF16, "M2_dlt")
    brow = k.sb((1, 2 * TC), BF16, "M2_brow")
    bb = k.sb((128, 2 * TC), BF16, "M2_bb")
    cb_ = k.sb((128, 2 * TC), BF16, "M2_cb")
    delta = k.sb((128, 4 * TC), F32, "M2_del")
    wdh = k.sb((128, 4 * TC), BF16, "M2_w")
    da = k.sb((128, 2 * TC), F32, "M2_da")
    dbu = k.sb((128, 2 * TC), BF16, "M2_dbu")
    st = k.sb((128, 2 * TC), F32, "M2_st")
    yac = k.sb((128, 4 * TC), F32, "M2_y")
    hch = k.sb((128, 4 * TC), BF16, "M2_h")
    rch = k.sb((128, TC), BF16, "M2_r")
    y2 = k.sb((128, 4 * TC), BF16, "M2_y2")
    carry = k.sb((128, 64), F32, "M2_cy")
    sfp = st
    xd_t = st[0:96, 0:TC]
    ones_row_b = k.sb((1, 128), BF16, "M2_orb")
    ones_row_f = k.sb((1, 128), F32, "M2_or")
    pool = [k.psum((128, 512), F32, f"M2_pp{i}") for i in range(4)]

    k.dma("sync", dtb_c[:, :], dtb[:, :])
    k.dma("sync", a_c[:, :], acol[:, :])
    k.act(a_c[:, :], a_c[:, :], AF.Exp)
    k.tsc(a_c[:, :], a_c[:, :], -1.0, ALU.mult)
    k.dma("sync", d_c[:, :], dcol[:, :])
    k.dma("sync", wstage[0:64, 0:512], dtw[:, :])
    k.copy(dtw_bf[:, :], wstage[0:64, 0:512], eng="vector")
    for c4 in range(4):
        k.dma("sync", wstage[:, :512], opw[c4 * 128:(c4 + 1) * 128, 0:512])
        k.copy(opw_bf[:, c4 * D: c4 * D + 512], wstage[:, :512], eng="vector")
        k.dma("sync", wstage[:, :512], opw[c4 * 128:(c4 + 1) * 128, 512:1024])
        k.copy(opw_bf[:, c4 * D + 512: (c4 + 1) * D], wstage[:, :512], eng="vector")
    k.memset(ones_row_f[:, :], 1.0)
    k.memset(ones_row_b[:, :], 1.0)
    k.memset(carry[:, :], 0.0)

    for tc in range(S // TC):
        tok = slice(tc * TC, (tc + 1) * TC)
        k.dma("sync", xd_sb[:, :], xds[0][:, tok])
        for i in range(1, 4):
            k.dma("gpsimd", xd_t[:, :], xds[i][:, tok])
            k.tt(xd_sb[:, :], xd_sb[:, :], xd_t[:, :], ALU.add)
        k.dma("scalar", bc_dram[:, tok], xd_sb[64:96, :])
        k.copy(dlt_bf[:, :], xd_sb[0:64, :], eng="vector")
        for c4 in range(4):
            dl = delta[:, c4 * TC:(c4 + 1) * TC]
            wv = wdh[:, c4 * TC:(c4 + 1) * TC]
            for sub in range(TC // 512):
                p = pool[sub % 2]
                k.mm(p[:, :512], dtw_bf[:, c4 * 128:(c4 + 1) * 128],
                     dlt_bf[:, sub * 512:(sub + 1) * 512], start=True, stop=True)
                k.tsc(dl[:, sub * 512:(sub + 1) * 512], p[:, :512],
                      dtb_c[:, c4:c4 + 1], ALU.add)
            sf = sfp[:, (c4 % 2) * TC:(c4 % 2) * TC + TC]
            k.act(sf, dl, AF.Abs)
            k.act(sf, sf, AF.Exp, scale=-1.0)
            k.act(sf, sf, AF.Ln, bias=1.0)
            k.tsc(dl, dl, 0.0, ALU.max)
            k.tt(dl, dl, sf, ALU.add)
            k.dma("sync", hch[:, c4 * TC:(c4 + 1) * TC],
                  h_q[c4 * 128:(c4 + 1) * 128, tok])
            k.tt(wv, dl, hch[:, c4 * TC:(c4 + 1) * TC], ALU.mult)
        for n in range(16):
            par = n % 2
            bbv = bb[:, par * TC:(par + 1) * TC]
            cbv = cb_[:, par * TC:(par + 1) * TC]
            for (row, dst) in ((n, bbv), (16 + n, cbv)):
                k.dma("gpsimd", brow[:, par * TC:par * TC + TC],
                      bc_dram[row: row + 1, tok])
                for sub in range(TC // 512):
                    k.mm(pool[2 + par][:, :512], ones_row_b[:, :],
                         brow[:, par * TC + sub * 512: par * TC + (sub + 1) * 512],
                         start=True, stop=True)
                    k.copy(dst[:, sub * 512:(sub + 1) * 512], pool[2 + par][:, :512],
                           eng="scalar")
            for c4 in range(4):
                p2 = (n * 4 + c4) % 2
                dl = delta[:, c4 * TC:(c4 + 1) * TC]
                wv = wdh[:, c4 * TC:(c4 + 1) * TC]
                dav = da[:, p2 * TC:(p2 + 1) * TC]
                dbv = dbu[:, p2 * TC:(p2 + 1) * TC]
                stv = st[:, p2 * TC:(p2 + 1) * TC]
                k.act(dav, dl, AF.Exp, scale=a_c[:, c4 * 16 + n: c4 * 16 + n + 1])
                k.tt(dbv, wv, bbv, ALU.mult)
                k.op("vector", "tensor_tensor_scan", reads=(dav, dbv, carry),
                     writes=(stv,), out=stv, data0=dav, data1=dbv,
                     initial=carry[:, c4 * 16 + n: c4 * 16 + n + 1],
                     op0=ALU.mult, op1=ALU.add)
                k.copy(carry[:, c4 * 16 + n: c4 * 16 + n + 1], stv[:, TC - 1: TC],
                       eng="vector")
                k.tt(dav, stv, cbv, ALU.mult, eng="gpsimd")
                yv = yac[:, c4 * TC:(c4 + 1) * TC]
                if n == 0:
                    k.copy(yv, dav, eng="gpsimd")
                else:
                    k.tt(yv, yv, dav, ALU.add, eng="gpsimd")
        for c4 in range(4):
            k.dma("sync", rch[:, :], res_q[c4 * 128:(c4 + 1) * 128, tok])
            hv = hch[:, c4 * TC:(c4 + 1) * TC]
            yv = yac[:, c4 * TC:(c4 + 1) * TC]
            sf = sfp[:, (c4 % 2) * TC:(c4 % 2) * TC + TC]
            k.stt(yv, hv, d_c[:, c4:c4 + 1], yv, ALU.mult, ALU.add)
            k.act(sf, rch[:, :], AF.Silu)
            k.tt(y2[:, c4 * TC:(c4 + 1) * TC], yv, sf, ALU.mult)
        for oc in range(8):
            for sub in range(TC // 512):
                p = pool[oc % 2]
                for c4 in range(4):
                    k.mm(p[:, :512],
                         opw_bf[:, c4 * D + oc * 128: c4 * D + (oc + 1) * 128],
                         y2[:, c4 * TC + sub * 512: c4 * TC + (sub + 1) * 512],
                         start=(c4 == 0), stop=(c4 == 3))
                k.copy(wstage[:, :512], p[:, :512], eng="scalar")
                k.dma("scalar",
                      mpart[oc * 128:(oc + 1) * 128,
                            tc * TC + sub * 512: tc * TC + (sub + 1) * 512],
                      wstage[:, :512])
    return k.finalize()


def run_M(x4_fm, W):
    maps = []
    ip = np.asarray(W["L1_attn_in_proj_w"], np.float32)
    cwf = np.asarray(W["L1_attn_conv_w"], np.float32)[:, 0, :]
    cbf = np.asarray(W["L1_attn_conv_b"], np.float32)
    xpw = np.asarray(W["L1_attn_x_proj_w"], np.float32)
    for c in range(NCORES):
        b, qd = c // 4, c % 4
        sl = slice(qd * 512, (qd + 1) * 512)
        m = {
            "x4full": x4_fm[b],
            "ipw_x": np.ascontiguousarray(ip[:, sl]),
            "ipw_r": np.ascontiguousarray(ip[:, 2048 + qd * 512: 2048 + (qd + 1) * 512]),
            "convw": np.ascontiguousarray(
                cwf[sl].reshape(4, 128, 4).transpose(1, 0, 2).reshape(128, 16)),
            "convb": np.ascontiguousarray(cbf[sl].reshape(4, 128).T),
            "xpw": np.ascontiguousarray(xpw[sl, :]),
        }
        maps.append(m)
    res = _run("M1", build_M1, maps)
    maps2 = []
    dtw = np.asarray(W["L1_attn_dt_proj_w"], np.float32)
    dtb = np.asarray(W["L1_attn_dt_proj_b"], np.float32)
    alog = np.asarray(W["L1_attn_A_log"], np.float32)
    dp = np.asarray(W["L1_attn_D_param"], np.float32)
    opw = np.asarray(W["L1_attn_out_proj_w"], np.float32)
    for c in range(NCORES):
        b, qd = c // 4, c % 4
        sl = slice(qd * 512, (qd + 1) * 512)
        aneg = -np.exp(alog[sl].astype(np.float64)).astype(np.float64)
        m = {
            "h_q": res[c]["h_q"],
            "res_q": res[c]["res_q"],
            "dtw": np.ascontiguousarray(dtw[:, sl]),
            "dtb": np.ascontiguousarray(dtb[sl].reshape(4, 128).T),
            "acol": np.ascontiguousarray(
                alog[sl].reshape(4, 128, 16).transpose(1, 0, 2).reshape(128, 64)),
            "dcol": np.ascontiguousarray(dp[sl].reshape(4, 128).T),
            "opw": np.ascontiguousarray(opw[sl, :]),
        }
        for i in range(4):
            m[f"xd{i}"] = res[b * 4 + i]["xdbl"]
        maps2.append(m)
    res2 = _run("M2", build_M2, maps2)
    return [[res2[b * 4 + i]["mpart"] for i in range(4)] for b in range(2)]


def kernel(**inputs):
    W = inputs
    x0 = np.asarray(inputs["L0_attn_x"], np.float32)
    x0_fm = [np.ascontiguousarray(x0[b].T) for b in range(2)]
    parts1 = run_F(x0_fm, W)
    x2_fm, _x1 = run_A(x0_fm, parts1, W)
    parts2 = run_F(x2_fm, W)
    x4_fm, _r3 = run_LT_chain(x2_fm, parts2, W)
    mparts = run_M(x4_fm, W)
    x6_fm, _r5 = run_LT_chain(x4_fm, mparts, W)
    out = np.stack([x6_fm[b].T for b in range(2)], axis=0)
    return np.ascontiguousarray(out, dtype=np.float32)
